# version 22
# baseline (speedup 1.0000x reference)
import contextlib

import numpy as np

import concourse.bass as bass
import concourse.mybir as mybir
from concourse.bass_utils import run_bass_kernel_spmd

F32 = mybir.dt.float32
BF16 = mybir.dt.bfloat16
AF = mybir.ActivationFunctionType
ALU = mybir.AluOpType
AX = mybir.AxisListType

NCORE = 8
T_CORE = 4096
D = 1024
NCH = 8
NSUP = 8
SUP = 512
NSUB = 4
NTOK_GLOBAL = 32768.0
LN_EPS = 1e-5
BN_EPS = 1e-5

DEBUG_OUTS = False


class _Op:
    __slots__ = ("stream", "sem", "fn", "deps", "ndma", "inc", "cum", "idx",
                 "dep_target")

    def __init__(self, stream, sem, fn, deps, ndma):
        self.stream, self.sem, self.fn = stream, sem, fn
        self.deps, self.ndma = deps, ndma
        self.inc = 0
        self.cum = None


class Plan:

    def __init__(self):
        self.ops = []
        self.last_by_stream = {}
        self._eng = None
        self._csem = None
        self._ccnt = None
        self._barrier_after = None

    def add(self, stream, fn, deps=(), ndma=0, sem=None):
        deps = [d for d in deps if d is not None]
        op = _Op(stream, sem or stream, fn, deps, ndma)
        op.idx = len(self.ops)
        self.ops.append(op)
        self.last_by_stream[stream] = op
        return op

    def barrier(self):
        lasts = dict(self.last_by_stream)
        outs = {}
        for st in ("T", "V", "S", "Y", "Q"):
            deps = [op for s2, op in lasts.items() if s2 != st]
            outs[st] = self.add(st, lambda eng: eng.nop(), deps=deps)
        return outs

    def finalize(self):
        needed = set()
        for op in self.ops:
            for d in op.deps:
                needed.add(id(d))
        cum = {}
        for op in self.ops:
            op.dep_target = id(op) in needed
            if op.ndma:
                amt = 16 * op.ndma
            elif op.dep_target:
                amt = 1
            else:
                amt = 0
            op.inc = amt
            if amt:
                cum[op.sem] = cum.get(op.sem, 0) + amt
                op.cum = cum[op.sem]

    def B(self):
        inst = self._proxy.last
        assert inst is not None, "B() with no prior instruction"
        self._ccnt[0] += 1
        inst.then_inc(self._csem, 1)
        self._eng.wait_ge(self._csem, self._ccnt[0])
        self._proxy.last = None

    def run(self, eng, stream, sems):
        waited = {}
        self._eng = eng
        self._csem = sems.get("c" + stream)
        self._ccnt = [0]
        self._proxy = _EngRecorder(eng)
        for op in self.ops:
            if op.stream != stream:
                continue
            for d in op.deps:
                if d.stream == stream and stream == "Y":
                    continue
                assert d.inc > 0, "dep target lacks inc"
                if waited.get(d.sem, -1) < d.cum:
                    eng.wait_ge(sems[d.sem], d.cum)
                    waited[d.sem] = d.cum
            ret = op.fn(self._proxy)
            if op.inc:
                if op.ndma:
                    insts = ret if isinstance(ret, (list, tuple)) else [ret]
                    assert len(insts) == op.ndma, "ndma mismatch"
                    for inst in insts:
                        inst.then_inc(sems[op.sem], 16)
                    if op.dep_target:
                        eng.wait_ge(sems[op.sem], op.cum)
                        waited[op.sem] = op.cum
                else:
                    inst = ret[-1] if isinstance(ret, (list, tuple)) else ret
                    assert inst is not None, "op fn returned nothing"
                    inst.then_inc(sems[op.sem], 1)


class _EngRecorder:

    def __init__(self, eng):
        self._eng = eng
        self.last = None

    def __getattr__(self, name):
        f = getattr(self._eng, name)
        if not callable(f):
            return f
        rec = self

        def wrap(*a, **kw):
            r = f(*a, **kw)
            if hasattr(r, "then_inc"):
                rec.last = r
            return r

        return wrap


class PsumRing:

    def __init__(self, tiles):
        self.halves = []
        for t in tiles:
            self.halves.append(t[:, 0:512])
            self.halves.append(t[:, 512:1024])
        self.tiles = tiles
        self.last = [None] * len(self.halves)
        self.i = 0

    def acquire_half(self):
        idx = self.i % len(self.halves)
        self.i += 1
        return self.halves[idx], [self.last[idx]], (idx,)

    def acquire_full(self):
        if self.i % 2:
            self.i += 1
        idx = self.i % len(self.halves)
        self.i += 2
        tile = self.tiles[idx // 2]
        return tile, [self.last[idx], self.last[idx + 1]], (idx, idx + 1)

    def done(self, token, op):
        for idx in token:
            self.last[idx] = op


class Ring2:

    def __init__(self, slots):
        self.slots = slots
        self.last = [None] * len(slots)
        self.i = 0

    def acquire(self):
        idx = self.i % len(self.slots)
        self.i += 1
        return self.slots[idx], [self.last[idx]], idx

    def done(self, idx, op):
        self.last[idx] = op


def build_program(debug=DEBUG_OUTS):
    nc = bass.Bass("TRN2", target_bir_lowering=False, detect_race_conditions=True)

    dt_in = lambda n, s: nc.dram_tensor(n, s, F32, kind="ExternalInput").ap()

    x_in = dt_in("x", [T_CORE, D])
    wqT = dt_in("wqT", [D, D])
    wkT = dt_in("wkT", [D, D])
    wvT = dt_in("wvT", [D, D])
    wpT = dt_in("wpT", [D, D])
    wfT = dt_in("wfT", [D, D])
    lora_up = dt_in("lora_up", [50, 4])
    up_auxT = dt_in("up_auxT", [50, D])
    down_aux = dt_in("down_aux", [100, D])
    lora_downT = dt_in("lora_downT", [100, 4])
    gb16 = dt_in("gb16", [128, 48])
    bcols = dt_in("bcols", [128, 16])
    eye_in = dt_in("eye", [128, 128])

    out_d = nc.dram_tensor("out", [T_CORE, D], F32, kind="ExternalOutput").ap()

    def scratch(name, shape, dtype):
        kind = "ExternalOutput" if debug else "Internal"
        return nc.dram_tensor(name, shape, dtype, kind=kind).ap()

    x1_d = scratch("x1_d", [NSUP, 128, NSUB * D], F32)
    hq_d = scratch("hq_d", [NSUP, 128, NCH * SUP], BF16)
    hk_d = scratch("hk_d", [NSUP, 128, NCH * SUP], BF16)
    hv_d = scratch("hv_d", [NSUP, 128, NCH * SUP], BF16)
    ccv_in = nc.dram_tensor("ccv_in", [2048], F32).ap()
    ccv_out = nc.dram_tensor("ccv_out", [2048], F32).ap()
    ccqk_in = nc.dram_tensor("ccqk_in", [4096], F32).ap()
    ccqk_out = nc.dram_tensor("ccqk_out", [4096], F32).ap()
    rowscr = nc.dram_tensor("rowscr", [2, D], F32).ap()

    plan = Plan()

    def chunked(dram_ap, p=128):
        return dram_ap.rearrange("(c p) t -> p c t", p=p)

    with contextlib.ExitStack() as es:
        E = es.enter_context

        w_half = E(nc.sbuf_tensor("w_half", [128, NCH * D], BF16))
        b_rep = E(nc.sbuf_tensor("b_rep", [128, D], BF16))
        bp_rep = E(nc.sbuf_tensor("bp_rep", [128, D], BF16))
        id_b = E(nc.sbuf_tensor("id_b", [128, 128], BF16))
        onesm = E(nc.sbuf_tensor("onesm", [128, 128], BF16))
        gbcols = E(nc.sbuf_tensor("gbcols", [128, 48], F32))
        alph = E(nc.sbuf_tensor("alph", [128, 16], F32))
        coff = E(nc.sbuf_tensor("coff", [128, 16], F32))
        av_c = E(nc.sbuf_tensor("av_c", [128, 8], F32))
        cv_c = E(nc.sbuf_tensor("cv_c", [128, 8], F32))
        neghalf = E(nc.sbuf_tensor("neghalf", [128, 16], F32))
        qacc = E(nc.sbuf_tensor("qacc", [128, 2 * 64], F32))
        kacc = E(nc.sbuf_tensor("kacc", [128, 2 * 64], F32))
        vacc = E(nc.sbuf_tensor("vacc", [128, 2 * 64], F32))
        ccv_sb = E(nc.sbuf_tensor("ccv_sb", [128, 16], F32))
        ccqk_sb = E(nc.sbuf_tensor("ccqk_sb", [128, 32], F32))
        st48v = E(nc.sbuf_tensor("st48v", [128, 16], F32))
        st48qk = E(nc.sbuf_tensor("st48qk", [128, 32], F32))
        VV = [E(nc.sbuf_tensor(f"VV{q}", [128, NCH * 128], BF16)) for q in range(2)]
        stt_sc = E(nc.sbuf_tensor("stt_sc", [128, 16], F32))
        stt2 = E(nc.sbuf_tensor("stt2", [128, 16], F32))
        wp_sb = E(nc.sbuf_tensor("wp_sb", [128, NCH * D], BF16))

        ph1 = contextlib.ExitStack()
        E1 = ph1.enter_context
        wq_sb = E1(nc.sbuf_tensor("wq_sb", [128, NCH * D], BF16))
        wk_sb = E1(nc.sbuf_tensor("wk_sb", [128, NCH * D], BF16))
        wv_sb = E1(nc.sbuf_tensor("wv_sb", [128, NCH * D], BF16))
        xbuf = [E1(nc.sbuf_tensor(f"xbuf{i}", [128, NSUB * D], F32)) for i in range(2)]
        ybuf = E1(nc.sbuf_tensor("ybuf", [128, NSUB * D], BF16))
        ybuf2p = E1(nc.sbuf_tensor("ybuf2p", [128, NSUB * D], BF16))
        y1T = E1(nc.sbuf_tensor("y1T", [128, NCH * SUP], BF16))
        y2T = E1(nc.sbuf_tensor("y2T", [128, NCH * SUP], BF16))
        hq_sb = E1(nc.sbuf_tensor("hq_sb", [128, NCH * SUP], BF16))
        hk_sb = E1(nc.sbuf_tensor("hk_sb", [128, NCH * SUP], BF16))
        hv_sb = E1(nc.sbuf_tensor("hv_sb", [128, NCH * SUP], BF16))
        junk = E1(nc.sbuf_tensor("junk", [128, 512], BF16))
        bc_sb = E1(nc.sbuf_tensor("bc_sb", [128, 16], F32))
        xps = E1(nc.sbuf_tensor("xps", [8, 256], F32))
        lnst = E1(nc.sbuf_tensor("lnst", [128, 48], F32))
        lnsm = E1(nc.sbuf_tensor("lnsm", [128, 10 * NSUB], F32))
        lnst_b = E1(nc.sbuf_tensor("lnst_b", [128, 48], F32))
        lnsm_b = E1(nc.sbuf_tensor("lnsm_b", [128, 10 * NSUB], F32))
        lnsc_a = E1(nc.sbuf_tensor("lnsc_a", [128, 16], F32))
        lnsc_b = E1(nc.sbuf_tensor("lnsc_b", [128, 16], F32))
        id_f = E1(nc.sbuf_tensor("id_f", [128, 128], F32))
        wstage = E1(nc.sbuf_tensor("wstage", [128, D], F32))
        wbig = [E1(nc.sbuf_tensor("wbig0", [128, 2 * D], F32))] * 2
        lup_sb = E1(nc.sbuf_tensor("lup_sb", [50, 4], F32))
        ldT_sb = E1(nc.sbuf_tensor("ldT_sb", [100, 4], F32))
        c_sb = E1(nc.sbuf_tensor("c_sb", [4, D], F32))
        a_sb = E1(nc.sbuf_tensor("a_sb", [128, 32], F32))
        at_sb = E1(nc.sbuf_tensor("at_sb", [4, D], F32))

        ps1 = contextlib.ExitStack()
        P1 = ps1.enter_context
        pp0 = P1(nc.psum_tensor("pp0", [128, D], F32))
        pp1 = P1(nc.psum_tensor("pp1", [128, D], F32))
        pp2x = P1(nc.psum_tensor("pp2x", [128, D], F32))
        trp = [P1(nc.psum_tensor(f"trp{i}", [128, 8 * 128], BF16)) for i in range(2)]

        pp = PsumRing([pp0, pp1, pp2x])
        trr = Ring2(trp)

        ld0 = plan.add("Y", lambda e: [
            e.dma_start(out=id_f[:, :], in_=eye_in),
            e.dma_start(out=gbcols[:, :], in_=gb16),
            e.dma_start(out=bc_sb[:, :], in_=bcols),
            e.dma_start(out=lup_sb[:, :], in_=lora_up),
            e.dma_start(out=wbig[0][0:50, 0:D], in_=up_auxT),
            e.dma_start(out=wbig[0][0:100, D:2 * D], in_=down_aux),
            e.dma_start(out=ldT_sb[:, :], in_=lora_downT),
        ], ndma=7)

        def consts_fn(e):
            e.tensor_copy(out=id_b[:, :], in_=id_f[:, :])
            e.memset(onesm[:, :], 1.0)
            e.memset(neghalf[:, :], -0.5)
            return e.tensor_scalar(out=bc_sb[:, 0:8], in0=bc_sb[:, 0:8], scalar1=0.5,
                                   scalar2=None, op0=ALU.mult)

        consts = plan.add("V", consts_fn, deps=[ld0])

        x_ch = chunked(x_in)
        ldx_ops = [None] * NSUP
        xb3_0 = xbuf[0][:, :].rearrange("p (a d) -> p a d", a=NSUB)
        ldx_ops[0] = plan.add("S", lambda e: e.dma_start(
            out=xb3_0, in_=x_ch[:, 0:NSUB, :]), ndma=1, sem="Z")


        mmC = plan.add("T", lambda e: [
            e.matmul(out=pp0[0:4, 0:512], lhsT=lup_sb[:, :], rhs=wbig[0][0:50, 0:512],
                     start=True, stop=True),
            e.matmul(out=pp0[0:4, 512:1024], lhsT=lup_sb[:, :],
                     rhs=wbig[0][0:50, 512:D], start=True, stop=True),
        ], deps=[ld0])
        evC = plan.add("V", lambda e: e.tensor_scalar(
            out=c_sb[:, :], in0=pp0[0:4, :], scalar1=0.5, scalar2=None, op0=ALU.mult),
            deps=[mmC])
        mmA = plan.add("T", lambda e: [
            e.matmul(out=pp1[:, i * 4:(i + 1) * 4],
                     lhsT=wbig[0][0:100, D + i * 128:D + (i + 1) * 128],
                     rhs=ldT_sb[:, :], start=True, stop=True, skip_group_check=True)
            for i in range(NCH)
        ], deps=[ld0])
        evA = plan.add("V", lambda e: e.tensor_copy(out=a_sb[:, :], in_=pp1[:, 0:32]),
                       deps=[mmA])
        mmAT = plan.add("T", lambda e: [
            e.transpose(out=pp0[0:4, i * 128:(i + 1) * 128],
                        in_=a_sb[:, i * 4:(i + 1) * 4], identity=id_f[:, :])
            for i in range(NCH)
        ], deps=[evA, evC])
        evAT = plan.add("V", lambda e: e.tensor_copy(out=at_sb[:, :], in_=pp0[0:4, :]),
                        deps=[mmAT])

        stg_bufs = [wbig[0][:, :], hq_sb[:, :].bitcast(F32),
                    hk_sb[:, :].bitcast(F32), hv_sb[:, :].bitcast(F32)]
        buf_last = [[mmC, mmA], [], [], []]

        def stage_q(w_dram, h, b, queue):
            deps = list(buf_last[b])
            op = plan.add(queue, lambda e, w_dram=w_dram, h=h, b=b: e.dma_start(
                out=stg_bufs[b].rearrange("p (c o) -> p c o", c=2),
                in_=w_dram[h * 256:(h + 1) * 256, :].rearrange(
                    "(c p) o -> p c o", p=128)),
                deps=deps, ndma=1, sem="Z" if queue == "S" else None)
            return op

        wf_loads = []
        for h in range(4):
            wf_loads.append(stage_q(wfT, h, h, "Y" if h % 2 == 0 else "Q"))

        bias_tr = plan.add("T", lambda e: [
            e.transpose(out=pp0[0:8, 0:128], in_=bc_sb[:, 0:8], identity=id_f[:, :]),
            e.transpose(out=pp0[0:8, 128:256], in_=bc_sb[:, 8:16], identity=id_f[:, :]),
        ], deps=[consts, evAT])
        bias_ev = plan.add("V", lambda e: e.tensor_copy(out=xps[:, :], in_=pp0[0:8, 0:256]),
                           deps=[bias_tr])
        rep_out = plan.add("Q", lambda e: [
            e.dma_start(out=rowscr[0:1, :].rearrange("a (c w) -> (a c) w", c=8),
                        in_=xps[:, 0:128]),
            e.dma_start(out=rowscr[1:2, :].rearrange("a (c w) -> (a c) w", c=8),
                        in_=xps[:, 128:256]),
        ], deps=[bias_ev], ndma=2)
        rep_back1 = plan.add("Q", lambda e: e.dma_start(
            out=wstage[:, :], in_=rowscr[0:1, :].partition_broadcast(128)),
            deps=[rep_out], ndma=1)
        rep_fin1 = plan.add("V", lambda e: e.tensor_scalar(
            out=b_rep[:, :], in0=wstage[:, :], scalar1=1.0 / 128,
            scalar2=None, op0=ALU.mult), deps=[rep_back1])
        rep_back2 = plan.add("Q", lambda e: e.dma_start(
            out=wstage[:, :], in_=rowscr[1:2, :].partition_broadcast(128)),
            deps=[rep_fin1], ndma=1)
        rep_fin = plan.add("V", lambda e: e.tensor_scalar(
            out=bp_rep[:, :], in0=wstage[:, :], scalar1=1.0 / 128,
            scalar2=None, op0=ALU.mult), deps=[rep_back2])

        xb3_1 = xbuf[1][:, :].rearrange("p (a d) -> p a d", a=NSUB)
        ldx_ops[1] = plan.add("Q", lambda e: e.dma_start(
            out=xb3_1, in_=x_ch[:, NSUB:2 * NSUB, :]), ndma=1)

        stt_hist = []
        for h in range(4):
            for i in range(h * 2, h * 2 + 2):
                tile = pp1 if i % 2 else pp0
                src = stg_bufs[h][:, (i % 2) * D:(i % 2 + 1) * D]
                mmL = plan.add("T", lambda e, i=i, tile=tile: [
                    e.matmul(out=tile[:, 0:512], lhsT=at_sb[:, i * 128:(i + 1) * 128],
                             rhs=c_sb[:, 0:512], start=True, stop=True,
                             skip_group_check=True),
                    e.matmul(out=tile[:, 512:1024], lhsT=at_sb[:, i * 128:(i + 1) * 128],
                             rhs=c_sb[:, 512:1024], start=True, stop=True,
                             skip_group_check=True),
                ], deps=[evAT]
                    + ([bias_ev] if i == 0 else [])
                    + ([evA] if i == 1 else [])
                    + ([stt_hist[i - 2]] if i >= 2 else []))
                stt = plan.add("V", lambda e, src=src, i=i, tile=tile:
                               e.scalar_tensor_tensor(
                                   out=w_half[:, i * D:(i + 1) * D], in0=src,
                                   scalar=0.5, in1=tile[:, :], op0=ALU.mult,
                                   op1=ALU.add),
                               deps=[wf_loads[h], mmL])
                stt_hist.append(stt)
            buf_last[h] = [stt_hist[-1]]
        whalf_ready = stt_hist[-1]

        w_ready = {}
        qk_loads = {}
        for wname, w_dram in (("q", wqT), ("k", wkT)):
            qk_loads[wname] = [
                stage_q(w_dram, h, h, "Y" if h % 2 == 0 else "S")
                for h in range(4)]

        def emit_cast(wname, w_sb, h, b, ld):
            cast = plan.add("V", lambda e, h=h, b=b, w_sb=w_sb: e.tensor_copy(
                out=w_sb[:, h * 2 * D:(h + 1) * 2 * D],
                in_=stg_bufs[b][:, :]), deps=[ld])
            buf_last[b] = [cast]
            w_ready[wname] = cast
            return cast

        for h in range(4):
            emit_cast("q", wq_sb, h, h, qk_loads["q"][h])
        for h in range(4):
            qk_loads["k"][h].deps.append(buf_last[h][0])

        stg_guards = {}
        vbufs = (0, 3, 0, 3)

        def emit_k_casts_and_v_loads():
            for h in range(4):
                emit_cast("k", wk_sb, h, h, qk_loads["k"][h])
            stg_guards["q"] = buf_last[1][0]
            qk_loads["v"] = [
                stage_q(wvT, h, vbufs[h], "Y" if h % 2 == 0 else "Q")
                for h in range(4)]
            for h in range(4):
                cast = emit_cast("v", wv_sb, h, vbufs[h], qk_loads["v"][h])
                if h + 2 < 4:
                    qk_loads["v"][h + 2].deps.append(cast)

        def emit_wp_staging():
            for h in range(4):
                ld = stage_q(wpT, h, 0, "Y")
                emit_cast("wp", wp_sb, h, 0, ld)

        pp.last = [stt_hist[6], stt_hist[6], stt_hist[7], stt_hist[7],
                   None, None]

        self_B = plan.B

        def make_ln(lnst_t, lnsm_t, sc_t):
            msum = lnsm_t[:, 0:4]
            m2sum = lnsm_t[:, 4:8]
            msqs = lnsm_t[:, 8:12]
            mean4 = lnsm_t[:, 12:16]
            vare = lnsm_t[:, 16:20]
            mean2 = lnsm_t[:, 20:24]
            rs4 = lnsm_t[:, 32:36]
            nmrs = lnsm_t[:, 36:40]
            st3 = lnst_t[:, :].rearrange("p (a g k) -> p a g k", a=NSUB, k=3)

            def stats_sub_fn(xb, a):
                def fn(e):
                    e.bn_stats(out=lnst_t[:, a * 12:a * 12 + 6],
                               in_=xb[:, a * D:a * D + 512])
                    return e.bn_stats(out=lnst_t[:, a * 12 + 6:a * 12 + 12],
                                      in_=xb[:, a * D + 512:(a + 1) * D])

                return fn

            def comb_fn():
                def fn(e):
                    for a in range(NSUB):
                        means = st3[:, a, :, 1]
                        m2s = st3[:, a, :, 2]
                        e.reduce_sum(out=msum[:, a:a + 1], in_=means, axis=AX.X)
                        e.reduce_sum(out=m2sum[:, a:a + 1], in_=m2s, axis=AX.X)
                        e.scalar_tensor_tensor(
                            out=sc_t[:, a * 4:(a + 1) * 4], in0=means, scalar=1.0,
                            in1=means, op0=ALU.mult, op1=ALU.mult,
                            accum_out=msqs[:, a:a + 1])
                    self_B()
                    e.tensor_scalar(out=mean4, in0=msum, scalar1=0.25,
                                    scalar2=None, op0=ALU.mult)
                    e.tensor_scalar(out=vare, in0=m2sum, scalar1=1.0 / 1024,
                                    scalar2=LN_EPS, op0=ALU.mult, op1=ALU.add)
                    self_B()
                    e.tensor_tensor(out=mean2, in0=mean4, in1=mean4, op=ALU.mult)
                    e.scalar_tensor_tensor(out=vare, in0=msqs, scalar=0.25, in1=vare,
                                           op0=ALU.mult, op1=ALU.add)
                    self_B()
                    return e.tensor_tensor(out=vare, in0=vare, in1=mean2,
                                           op=ALU.subtract)

                return fn

            def stats_fn(xb):
                subs = [stats_sub_fn(xb, a) for a in range(NSUB)]
                comb = comb_fn()

                def fn(e):
                    for sf in subs:
                        sf(e)
                    self_B()
                    return comb(e)

                return fn

            def comb_sub_fn(a):
                def fn(e):
                    means = st3[:, a, :, 1]
                    m2s = st3[:, a, :, 2]
                    e.reduce_sum(out=msum[:, a:a + 1], in_=means, axis=AX.X)
                    e.reduce_sum(out=m2sum[:, a:a + 1], in_=m2s, axis=AX.X)
                    e.scalar_tensor_tensor(
                        out=sc_t[:, a * 4:(a + 1) * 4], in0=means, scalar=1.0,
                        in1=means, op0=ALU.mult, op1=ALU.mult,
                        accum_out=msqs[:, a:a + 1])
                    self_B()
                    e.tensor_scalar(out=mean4[:, a:a + 1], in0=msum[:, a:a + 1],
                                    scalar1=0.25, scalar2=None, op0=ALU.mult)
                    e.tensor_scalar(out=vare[:, a:a + 1], in0=m2sum[:, a:a + 1],
                                    scalar1=1.0 / 1024, scalar2=LN_EPS,
                                    op0=ALU.mult, op1=ALU.add)
                    self_B()
                    e.tensor_tensor(out=mean2[:, a:a + 1], in0=mean4[:, a:a + 1],
                                    in1=mean4[:, a:a + 1], op=ALU.mult)
                    e.scalar_tensor_tensor(out=vare[:, a:a + 1], in0=msqs[:, a:a + 1],
                                           scalar=0.25, in1=vare[:, a:a + 1],
                                           op0=ALU.mult, op1=ALU.add)
                    self_B()
                    return e.tensor_tensor(out=vare[:, a:a + 1], in0=vare[:, a:a + 1],
                                           in1=mean2[:, a:a + 1], op=ALU.subtract)

                return fn

            def rs_sub_fn(a):
                def fn(e):
                    return e.tensor_tensor(out=rs4[:, a:a + 1], in0=vare[:, a:a + 1],
                                           in1=neghalf[:, 0:1], op=ALU.pow)

                return fn

            def fin_sub_fn(xb, ydst, a):
                def fn(e):
                    e.scalar_tensor_tensor(out=nmrs[:, a:a + 1], in0=mean4[:, a:a + 1],
                                           scalar=-1.0, in1=rs4[:, a:a + 1],
                                           op0=ALU.mult, op1=ALU.mult)
                    self_B()
                    return e.tensor_scalar(
                        out=ydst[:, a * D:(a + 1) * D], in0=xb[:, a * D:(a + 1) * D],
                        scalar1=rs4[:, a:a + 1], scalar2=nmrs[:, a:a + 1],
                        op0=ALU.mult, op1=ALU.add)

                return fn

            def nmrs_sub_fn(a):
                def fn(e):
                    return e.scalar_tensor_tensor(
                        out=nmrs[:, a:a + 1], in0=mean4[:, a:a + 1], scalar=-1.0,
                        in1=rs4[:, a:a + 1], op0=ALU.mult, op1=ALU.mult)

                return fn

            def apply_sub_fn(xb, ydst, a):
                def fn(e):
                    return e.activation(
                        out=ydst[:, a * D:(a + 1) * D],
                        in_=xb[:, a * D:(a + 1) * D], func=AF.Identity,
                        scale=rs4[:, a:a + 1], bias=nmrs[:, a:a + 1])

                return fn

            def rs_fn(e):
                return e.tensor_tensor(out=rs4, in0=vare, in1=neghalf[:, 0:4],
                                       op=ALU.pow)

            def fin_fn(xb, ydst):
                def fn(e):
                    e.scalar_tensor_tensor(out=nmrs, in0=mean4, scalar=-1.0, in1=rs4,
                                           op0=ALU.mult, op1=ALU.mult)
                    self_B()
                    insts = []
                    for a in range(NSUB):
                        insts.append(e.tensor_scalar(
                            out=ydst[:, a * D:(a + 1) * D], in0=xb[:, a * D:(a + 1) * D],
                            scalar1=rs4[:, a:a + 1], scalar2=nmrs[:, a:a + 1],
                            op0=ALU.mult, op1=ALU.add))
                    return insts

                return fn

            return (stats_fn, rs_fn, fin_fn, stats_sub_fn, comb_fn,
                    comb_sub_fn, rs_sub_fn, fin_sub_fn, nmrs_sub_fn,
                    apply_sub_fn)

        (ln_stats_fn, ln_rs_fn, ln_fin_fn, ln_stats_sub_fn, ln_comb_fn,
         _, _, _, _, _) = make_ln(lnst, lnsm, lnsc_a)
        (_, _, _, ln2_stats_sub_fn, _, ln2_comb_sub_fn, ln2_rs_sub_fn,
         ln2_fin_sub_fn, _, _) = make_ln(lnst_b, lnsm_b, lnsc_b)

        def emit_transposes(ydst, yTdst, dep_fin, dep_reuse):
            evs = []
            for a in range(NSUB):
                bank, bdeps, bidx = trr.acquire()
                dfin = dep_fin[a] if isinstance(dep_fin, list) else dep_fin
                tr = plan.add("T", lambda e, a=a, bank=bank, ydst=ydst: [
                    e.transpose(out=bank[:, i * 128:(i + 1) * 128],
                                in_=ydst[:, a * D + i * 128:a * D + (i + 1) * 128],
                                identity=id_b[:, :])
                    for i in range(NCH)
                ], deps=[dfin] + bdeps + ([dep_reuse] if a == 0 else []))
                ev = plan.add("V", lambda e, a=a, bank=bank, yTdst=yTdst: e.tensor_copy(
                    out=yTdst[:, :].rearrange("p (i t) -> p i t", i=NCH)[:, :, a * 128:(a + 1) * 128],
                    in_=bank[:, :].rearrange("p (i t) -> p i t", i=NCH)),
                    deps=[tr])
                trr.done(bidx, ev)
                evs.append(ev)
            return evs

        fin2_ops = [None] * NSUP
        spx1_ops = [None] * NSUP
        sp_hq = [None] * NSUP
        sp_hk = [None] * NSUP
        sp_hv = [None] * NSUP
        last_ffn_mm = None
        last_brv_mm = None
        last_tr2 = None
        last_evq = {"q": None, "k": None}
        last_vst2 = None
        last_vst1 = None
        last_evq_all = []
        last_evk_all = []
        last_evv_all = []
        last_junk = [None]
        out_ch = chunked(out_d)

        fin1_ops = [None] * NSUP
        ev1_ops = [None] * NSUP

        rs1_ops = [None] * NSUP

        def emit_head_stats(s):
            if rs1_ops[s] is not None:
                return
            xb = xbuf[s % 2]
            xb3 = xb[:, :].rearrange("p (a d) -> p a d", a=NSUB)
            if ldx_ops[s] is None:
                ldx_ops[s] = plan.add("Y", lambda e, s=s, xb3=xb3: e.dma_start(
                    out=xb3, in_=x_ch[:, s * NSUB:(s + 1) * NSUB, :]),
                    deps=[fin2_ops[s - 2] if s >= 2 else None,
                          spx1_ops[s - 2] if s >= 2 else None],
                    ndma=1)
            st1 = plan.add("V", ln_stats_fn(xb), deps=[ldx_ops[s]])
            rs1_ops[s] = plan.add("Q", ln_rs_fn,
                                  deps=[st1, consts if s == 0 else None], sem="G")

        def emit_head_fin(s):
            xb = xbuf[s % 2]
            fin1_ops[s] = plan.add("V", ln_fin_fn(xb, ybuf),
                                   deps=[rs1_ops[s],
                                         ev1_ops[s - 1][3] if s >= 1 else None])

        def emit_head_tr(s):
            ev1_ops[s] = emit_transposes(ybuf, y1T, fin1_ops[s], last_ffn_mm)

        emit_head_stats(0)
        emit_head_fin(0)
        emit_head_tr(0)
        ffn_mm_ops = [[] for _ in range(NSUP)]

        def emit_ffn_mm_one(s, a):
            ev1 = ev1_ops[s]
            tile, tdeps, tok = pp.acquire_full()
            mm = plan.add("T", lambda e, a=a, tile=tile: [
                e.matmul(out=tile[:, oh * 512:(oh + 1) * 512],
                         lhsT=y1T[:, i * SUP + a * 128:i * SUP + (a + 1) * 128],
                         rhs=w_half[:, i * D + oh * 512:i * D + oh * 512 + 512],
                         start=(i == 0), stop=False)
                for oh in range(2) for i in range(NCH)
            ] + [
                e.matmul(out=tile[:, oh * 512:(oh + 1) * 512], lhsT=onesm[:, :],
                         rhs=b_rep[:, oh * 512:(oh + 1) * 512], start=False, stop=True)
                for oh in range(2)
            ], deps=[ev1[a]] + tdeps
                + ([whalf_ready, rep_fin, consts] if (s == 0 and a == 0) else []))
            ffn_mm_ops[s].append((mm, tile, tok))
            return mm

        for a0 in range(3):
            last_ffn_mm = emit_ffn_mm_one(0, a0)
        ffn_cursor = [3]
        epi_ops = [[None] * NSUB for _ in range(NSUP)]

        def emit_epi(s, a, pre=False):
            mm, tile, tok = ffn_mm_ops[s][a]
            xb = xbuf[s % 2]
            epi = plan.add("V", lambda e, a=a, tile=tile, xb=xb: [
                e.tensor_tensor(out=xb[:, a * D + oh * 512:a * D + (oh + 1) * 512],
                                in0=tile[:, oh * 512:(oh + 1) * 512],
                                in1=xb[:, a * D + oh * 512:a * D + (oh + 1) * 512],
                                op=ALU.add)
                for oh in range(2)
            ], deps=[mm] + ([fin1_ops[s]] if pre else []))
            pp.done(tok, epi)
            epi_ops[s][a] = epi
            return epi

        ln2fin_ops = [[None] * NSUB for _ in range(NSUP)]

        def emit_ln2_chain(s, a, pre_tr2=None):
            xb = xbuf[s % 2]
            sta = plan.add("V", ln2_stats_sub_fn(xb, a), deps=[epi_ops[s][a]])
            cmb = plan.add("V", ln2_comb_sub_fn(a), deps=[sta])
            rsx = plan.add("Q", ln2_rs_sub_fn(a), deps=[cmb], sem="G")
            fin = plan.add("V", ln2_fin_sub_fn(xb, ybuf2p, a),
                           deps=[rsx, pre_tr2])
            ln2fin_ops[s][a] = fin
            return fin

        for s in range(NSUP):
            xb = xbuf[s % 2]
            xb3 = xb[:, :].rearrange("p (a d) -> p a d", a=NSUB)
            if s + 1 < NSUP:
                emit_head_stats(s + 1)

            for a in range(NSUB):
                if ln2fin_ops[s][a] is None:
                    if epi_ops[s][a] is None:
                        emit_epi(s, a)
                    emit_ln2_chain(s, a, pre_tr2=last_tr2 if a == 0 else None)
                p = ffn_cursor[0]
                if p < NSUP * NSUB:
                    last_ffn_mm = emit_ffn_mm_one(p // NSUB, p % NSUB)
                    ffn_cursor[0] = p + 1
                if a == 0 and s + 1 < NSUP:
                    if fin1_ops[s + 1] is None:
                        emit_head_fin(s + 1)
                    emit_head_tr(s + 1)

            if s + 1 < NSUP:
                for a_n in range(3):
                    emit_epi(s + 1, a_n, pre=True)

            spx1 = plan.add("Q", lambda e, s=s, xb=xb: e.dma_start(
                out=x1_d[s], in_=xb[:, :]),
                deps=epi_ops[s], ndma=1)
            spx1_ops[s] = spx1

            fin2_ops[s] = ln2fin_ops[s][3]
            ev2 = emit_transposes(ybuf2p, y2T, ln2fin_ops[s], last_brv_mm)
            last_tr2 = ev2[3]

            if s == 0:
                emit_k_casts_and_v_loads()
            if s == 1:
                emit_wp_staging()

            if s + 2 < NSUP:
                emit_head_stats(s + 2)
            if s + 1 < NSUP:
                for a_n in range(3):
                    emit_ln2_chain(s + 1, a_n,
                                   pre_tr2=last_tr2 if a_n == 0 else None)
                if s + 2 < NSUP:
                    emit_head_fin(s + 2)

            br_last = {}
            for br, w_sb, h_sb, acc, sp_list, ev_all in (
                ("v", wv_sb, hv_sb, vacc, sp_hv, last_evv_all),
                ("q", wq_sb, hq_sb, qacc, sp_hq, last_evq_all),
                ("k", wk_sb, hk_sb, kacc, sp_hk, last_evk_all),
            ):
                evs = []
                for o in range(NCH):
                    half, hdeps, htok = pp.acquire_half()
                    mm = plan.add("T", lambda e, o=o, half=half, w_sb=w_sb: [
                        e.matmul(out=half,
                                 lhsT=w_sb[:, i * D + o * 128:i * D + (o + 1) * 128],
                                 rhs=y2T[:, i * SUP:(i + 1) * SUP],
                                 start=(i == 0), stop=(i == NCH - 1))
                        for i in range(NCH)
                    ], deps=[ev2[3]] + hdeps
                        + ([sp_list[s - 1]] if (o == 0 and s >= 1) else [])
                        + ([w_ready[br]] if (o == 0 and s == 0) else []))
                    ev = plan.add("S", lambda e, o=o, s=s, half=half, h_sb=h_sb, acc=acc: [
                        e.activation(out=h_sb[:, o * SUP:(o + 1) * SUP], in_=half,
                                     func=AF.Copy, accum_out=acc[:, o * 8 + s:o * 8 + s + 1]),
                        e.activation(out=junk[:, :], in_=half, func=AF.Square,
                                     accum_out=acc[:, 64 + o * 8 + s:64 + o * 8 + s + 1]),
                    ], deps=[mm, last_junk[0]]
                        + ([stg_guards["q"]] if (s == 0 and br == "q"
                                                 and o == 0) else []))
                    last_junk[0] = ev
                    pp.done(htok, ev)
                    evs.append(ev)
                ev_all.append(evs[-1])
                br_last[br] = evs[-1]
                if s == NSUP - 1 and br == "v":
                    red_v = plan.add("V", lambda e: [
                        e.reduce_sum(out=ccv_sb[:, g * 8 + o:g * 8 + o + 1],
                                     in_=vacc[:, :].rearrange(
                                         "p (g o s) -> p g o s",
                                         g=2, s=8)[:, g, o, :],
                                     axis=AX.X)
                        for g in range(2) for o in range(NCH)
                    ], deps=[br_last["v"]])
                    ccv_dma = plan.add("S", lambda e: e.dma_start(
                        out=ccv_in[:].rearrange("(p c) -> p c", c=16),
                        in_=ccv_sb[:, :]), deps=[red_v], ndma=1, sem="W")
                    coll_v = plan.add("Q", lambda e: e.collective_compute(
                        "AllReduce", ALU.add, replica_groups=[list(range(NCORE))],
                        ins=[ccv_in.opt()], outs=[ccv_out.opt()]),
                        deps=[ccv_dma], sem="Cv")
            if s == NSUP - 1:
                red_qk = plan.add("V", lambda e: [
                    e.reduce_sum(out=ccqk_sb[:, g * 16 + b * 8 + o:
                                             g * 16 + b * 8 + o + 1],
                                 in_=acc[:, :].rearrange("p (g o s) -> p g o s",
                                                         g=2, s=8)[:, g, o, :],
                                 axis=AX.X)
                    for g in range(2) for b, acc in ((0, qacc), (1, kacc))
                    for o in range(NCH)
                ], deps=[br_last["q"], br_last["k"]])
                ccqk_dma = plan.add("S", lambda e: e.dma_start(
                    out=ccqk_in[:].rearrange("(p c) -> p c", c=32),
                    in_=ccqk_sb[:, :]), deps=[red_qk], ndma=1, sem="W")
                coll_qk = plan.add("Q", lambda e: e.collective_compute(
                    "AllReduce", ALU.add, replica_groups=[list(range(NCORE))],
                    ins=[ccqk_in.opt()], outs=[ccqk_out.opt()]),
                    deps=[ccqk_dma], sem="Cq")

            sp = plan.add("Q", lambda e, s=s: [
                e.dma_start(out=hq_d[s], in_=hq_sb[:, :]),
                e.dma_start(out=hk_d[s], in_=hk_sb[:, :]),
                e.dma_start(out=hv_d[s], in_=hv_sb[:, :]),
            ], deps=[br_last["q"], br_last["k"], br_last["v"]], ndma=3)
            sp_hq[s] = sp_hk[s] = sp_hv[s] = sp

        ps1.close()
        ph1.close()
        plan.barrier()

        ccv_back = plan.add("Y", lambda e: e.dma_start(
            out=st48v[:, :], in_=ccv_out[:].rearrange("(p c) -> p c", c=16)),
            deps=[coll_v], ndma=1)

        S1 = stt2[:, 0:8]
        S2 = stt2[:, 8:16]

        def finv_a_fn(e):
            e.tensor_scalar(out=S1, in0=st48v[:, 0:8], scalar1=1.0 / NTOK_GLOBAL,
                            scalar2=None, op0=ALU.mult)
            e.tensor_scalar(out=cv_c[:, :], in0=st48v[:, 8:16],
                            scalar1=1.0 / NTOK_GLOBAL, scalar2=None, op0=ALU.mult)
            plan.B()
            e.tensor_tensor(out=S2, in0=S1, in1=S1, op=ALU.mult)
            plan.B()
            e.tensor_tensor(out=cv_c[:, :], in0=cv_c[:, :], in1=S2, op=ALU.subtract)
            plan.B()
            return e.tensor_scalar(out=cv_c[:, :], in0=cv_c[:, :], scalar1=BN_EPS,
                                   scalar2=None, op0=ALU.add)

        finv_a = plan.add("V", finv_a_fn, deps=[ccv_back])
        finv_b = plan.add("Q", lambda e: e.tensor_tensor(
            out=av_c[:, :], in0=cv_c[:, :], in1=neghalf[:, 0:8], op=ALU.pow),
            deps=[finv_a], sem="G")

        def finv_c_fn(e):
            e.scalar_tensor_tensor(out=av_c[:, :], in0=gbcols[:, 32:40], scalar=4.0,
                                   in1=av_c[:, :], op0=ALU.mult, op1=ALU.mult)
            plan.B()
            e.tensor_tensor(out=S1, in0=S1, in1=av_c[:, :], op=ALU.mult)
            plan.B()
            e.scalar_tensor_tensor(out=cv_c[:, :], in0=gbcols[:, 40:48], scalar=4.0,
                                   in1=S1, op0=ALU.mult, op1=ALU.subtract)
            plan.B()
            return e.tensor_scalar(out=cv_c[:, :], in0=cv_c[:, :], scalar1=-4.0,
                                   scalar2=None, op0=ALU.add)

        finv_c = plan.add("V", finv_c_fn, deps=[finv_b])

        mu16 = stt_sc[:, 0:16]

        def fin_a_fn(e):
            e.tensor_scalar(out=mu16, in0=st48qk[:, 0:16], scalar1=1.0 / NTOK_GLOBAL,
                            scalar2=None, op0=ALU.mult)
            e.tensor_scalar(out=coff[:, :], in0=st48qk[:, 16:32],
                            scalar1=1.0 / NTOK_GLOBAL, scalar2=None, op0=ALU.mult)
            plan.B()
            e.tensor_tensor(out=alph[:, :], in0=mu16, in1=mu16, op=ALU.mult)
            plan.B()
            e.tensor_tensor(out=coff[:, :], in0=coff[:, :], in1=alph[:, :],
                            op=ALU.subtract)
            plan.B()
            return e.tensor_scalar(out=coff[:, :], in0=coff[:, :], scalar1=BN_EPS,
                                   scalar2=None, op0=ALU.add)

        fin_c = None

        def emit_fin_qk():
            nonlocal fin_c
            ccqk_back = plan.add("Q", lambda e: e.dma_start(
                out=st48qk[:, :],
                in_=ccqk_out[:].rearrange("(p c) -> p c", c=32)),
                deps=[coll_qk], ndma=1)
            fin_a = plan.add("V", fin_a_fn, deps=[ccqk_back])
            fin_b = plan.add("Q", lambda e: e.tensor_tensor(
                out=alph[:, :], in0=coff[:, :], in1=neghalf[:, 0:16], op=ALU.pow),
                deps=[fin_a], sem="G")
            fin_c = plan.add("V", fin_c_fn, deps=[fin_b])

        def fin_c_fn(e):
            e.scalar_tensor_tensor(out=alph[:, :], in0=gbcols[:, 0:16], scalar=4.0,
                                   in1=alph[:, :], op0=ALU.mult, op1=ALU.mult)
            plan.B()
            e.tensor_tensor(out=mu16, in0=mu16, in1=alph[:, :], op=ALU.mult)
            plan.B()
            e.scalar_tensor_tensor(out=coff[:, :], in0=gbcols[:, 16:32], scalar=4.0,
                                   in1=mu16, op0=ALU.mult, op1=ALU.subtract)
            plan.B()
            return e.tensor_scalar(out=coff[:, :], in0=coff[:, :], scalar1=-4.0,
                                   scalar2=None, op0=ALU.add)

        ph2 = contextlib.ExitStack()
        E2 = ph2.enter_context
        hq_in = [E2(nc.sbuf_tensor(f"hq_in{i}", [128, NCH * SUP], BF16))
                 for i in range(2)]
        hk_in = [E2(nc.sbuf_tensor(f"hk_in{i}", [128, NCH * SUP], BF16))
                 for i in range(2)]
        hv_in = [E2(nc.sbuf_tensor(f"hv_in{i}", [128, NCH * SUP], BF16))
                 for i in range(2)]
        qbuf = [E2(nc.sbuf_tensor(f"qbuf{i}", [128, NCH * SUP], BF16))
                for i in range(2)]
        qk_ring = [E2(nc.sbuf_tensor(f"qk_r{i}", [128, NCH * SUP], BF16))
                   for i in range(3)]
        vfm_sb = [E2(nc.sbuf_tensor(f"vfm_sb{i}", [128, NCH * SUP], BF16))
                  for i in range(2)]
        v_tok = E2(nc.sbuf_tensor("v_tok", [128, NSUB * D], BF16))

        ps2b = contextlib.ExitStack()
        P2 = ps2b.enter_context
        at_ps = [P2(nc.psum_tensor(f"at_ps{i}", [128, 512], F32)) for i in range(2)]
        qp0 = P2(nc.psum_tensor("qp0", [128, D], F32))
        qp1 = P2(nc.psum_tensor("qp1", [128, D], F32))
        pp2 = PsumRing([qp0, qp1])

        ps2a = contextlib.ExitStack()
        vvps = [ps2a.enter_context(nc.psum_tensor(f"vvps{q}", [128, 512], F32))
                for q in range(2)]

        zeroVV = plan.add("V", lambda e: [
            e.memset(VV[0][:, :], 0.0),
            e.memset(VV[1][:, :], 0.0),
        ])

        x1b = [E2(nc.sbuf_tensor(f"x1b{i}", [128, NSUB * D], F32)) for i in range(2)]
        attn_sb = E2(nc.sbuf_tensor("attn_sb", [128, NCH * SUP], BF16))
        ybuf2 = E2(nc.sbuf_tensor("ybuf2", [128, NSUB * D], BF16))
        y3T = E2(nc.sbuf_tensor("y3T", [128, NCH * 256], BF16))
        ln2st = [E2(nc.sbuf_tensor(f"ln2st{i}", [128, 48], F32)) for i in range(2)]
        ln2sm = [E2(nc.sbuf_tensor(f"ln2sm{i}", [128, 10 * NSUB], F32))
                 for i in range(2)]
        ln3sc = [E2(nc.sbuf_tensor(f"ln3sc{i}", [128, 16], F32)) for i in range(2)]
        ln3 = [make_ln(ln2st[i], ln2sm[i], ln3sc[i]) for i in range(2)]

        qsig_ops = [None] * NSUP
        ksig_ops = [None] * NSUP
        qkm_ch = [None] * NSUP
        attn_ch = [None] * NSUP
        vsig_ops = [None] * NSUP
        vtr_ops = [None] * NSUP
        vvmm_ops = [None] * NSUP
        ev_vv_ops = [None, None]

        def emit_2a_v(s):
            seq = s // 4
            hv = hv_in[s % 2]
            vfm = vfm_sb[s % 2]
            vt = v_tok
            ldhv = plan.add("Y", lambda e, s=s, hv=hv: e.dma_start(
                out=hv[:, :], in_=hv_d[s]),
                deps=[vsig_ops[s - 2] if s >= 2 else None], ndma=1)
            vsg = plan.add("S", lambda e, hv=hv, vfm=vfm: [
                e.activation(out=vfm[:, o * SUP:(o + 1) * SUP],
                             in_=hv[:, o * SUP:(o + 1) * SUP], func=AF.Sigmoid,
                             scale=av_c[:, o:o + 1], bias=cv_c[:, o:o + 1])
                for o in range(NCH)
            ], deps=[ldhv, finv_c,
                     vtr_ops[s - 2][3] if s >= 2 else None])
            vsig_ops[s] = vsg
            vtrs = []
            for a in range(NSUB):
                bank, bdeps, bidx = pb_ring.acquire()
                bankb = bank[:, :].bitcast(BF16)
                tr = plan.add("T", lambda e, a=a, bankb=bankb, vfm=vfm: [
                    e.transpose(out=bankb[:, c * 128:(c + 1) * 128],
                                in_=vfm[:, c * SUP + a * 128:c * SUP + (a + 1) * 128],
                                identity=id_b[:, :])
                    for c in range(NCH)
                ], deps=[vsg] + bdeps)
                ev = plan.add("V", lambda e, a=a, bankb=bankb, vt=vt: e.tensor_copy(
                    out=vt[:, a * D:(a + 1) * D], in_=bankb[:, :]),
                    deps=[tr]
                    + ([vvmm_ops[s - 1]] if (a == 0 and s >= 1) else []))
                pb_ring.done(bidx, ev)
                vtrs.append(ev)
            vtr_ops[s] = vtrs

            first_in_seq = (s % 4 == 0)
            vvmm = plan.add("T", lambda e, s=s, seq=seq, first=first_in_seq, vt=vt: [
                e.matmul(out=vvps[seq][(0 if h == 0 else 64):(64 if h == 0 else 128),
                                       j * 64:(j + 1) * 64],
                         lhsT=vt[:, a * D + j * 128 + h * 64:a * D + j * 128 + (h + 1) * 64],
                         rhs=vt[:, a * D + j * 128 + h * 64:a * D + j * 128 + (h + 1) * 64],
                         start=(first and a == 0 and j == 0), stop=False,
                         skip_group_check=True,
                         tile_position=(0, 0 if h == 0 else 64))
                for a in range(NSUB) for j in range(NCH) for h in range(2)
            ], deps=vtrs)
            vvmm_ops[s] = vvmm

            if s % 4 == 3:
                evv = plan.add("V", lambda e, seq=seq: [
                    e.tensor_scalar(
                        out=VV[seq][:, :].rearrange("p (j e) -> p j e", j=NCH)[0:64, :, 0:64],
                        in0=vvps[seq][0:64, :].rearrange("p (j e) -> p j e", j=NCH),
                        scalar1=0.125, scalar2=None, op0=ALU.mult),
                    e.tensor_scalar(
                        out=VV[seq][:, :].rearrange("p (j e) -> p j e", j=NCH)[64:128, :, 64:128],
                        in0=vvps[seq][64:128, :].rearrange("p (j e) -> p j e", j=NCH),
                        scalar1=0.125, scalar2=None, op0=ALU.mult),
                ], deps=[vvmm, zeroVV])
                ev_vv_ops[seq] = evv

        def emit_2a_qk(s):
            qk = qk_ring[s % 3]
            ldhqk = plan.add("Y", lambda e, s=s: [
                e.dma_start(out=hq_in[s % 2][:, :], in_=hq_d[s]),
                e.dma_start(out=hk_in[s % 2][:, :], in_=hk_d[s]),
            ], deps=[qsig_ops[s - 2] if s >= 2 else None,
                     ksig_ops[s - 2] if s >= 2 else None], ndma=2)
            qkm_ch[s] = [None] * NCH
            qb = qbuf[s % 2]
            for o in range(NCH):
                qsig = plan.add("S", lambda e, s=s, o=o, qb=qb: e.activation(
                    out=qb[:, o * SUP:(o + 1) * SUP],
                    in_=hq_in[s % 2][:, o * SUP:(o + 1) * SUP], func=AF.Sigmoid,
                    scale=alph[:, o:o + 1], bias=coff[:, o:o + 1]),
                    deps=[ldhqk if o == 0 else None,
                          fin_c if (s == 0 and o == 0) else None,
                          qkm_ch[s - 2][o] if s >= 2 else None])
                ksig = plan.add("S", lambda e, s=s, o=o, qk=qk: e.activation(
                    out=qk[:, o * SUP:(o + 1) * SUP],
                    in_=hk_in[s % 2][:, o * SUP:(o + 1) * SUP], func=AF.Sigmoid,
                    scale=alph[:, 8 + o:9 + o], bias=coff[:, 8 + o:9 + o]),
                    deps=[attn_ch[s - 3][o] if s >= 3 else None])
                qkm_ch[s][o] = plan.add("V", lambda e, o=o, qk=qk, qb=qb: e.tensor_tensor(
                    out=qk[:, o * SUP:(o + 1) * SUP],
                    in0=qk[:, o * SUP:(o + 1) * SUP],
                    in1=qb[:, o * SUP:(o + 1) * SUP], op=ALU.mult),
                    deps=[qsig, ksig])
            qsig_ops[s] = qsig
            ksig_ops[s] = ksig

        st_out_ops = [None] * NSUP
        last_wp_mm = None
        last_tr3 = None
        pb_ring = Ring2(at_ps)
        epi2_by_sup = [None] * NSUP
        st3s_by_sup = [None] * NSUP
        ldx1_by_sup = [None] * NSUP
        last_ffn2_mm = None

        def emit_2b_front(s):
            nonlocal last_wp_mm
            seq = s // 4
            qk = qk_ring[s % 3]
            x1 = x1b[s % 2]
            ldx1_by_sup[s] = plan.add("Y", lambda e, s=s, x1=x1: e.dma_start(
                out=x1[:, :], in_=x1_d[s]),
                deps=[spx1_ops[s]]
                + (list(st_out_ops[s - 2]) if s >= 2 else []), ndma=1)

            ev_ats = []
            st3s = []
            attn_ch[s] = [None] * NCH
            for j in range(NCH):
                bank, bdeps, bidx = pb_ring.acquire()
                amm = plan.add("T", lambda e, j=j, seq=seq, bank=bank, qk=qk: e.matmul(
                    out=bank[:, :], lhsT=VV[seq][:, j * 128:(j + 1) * 128],
                    rhs=qk[:, j * SUP:(j + 1) * SUP], start=True, stop=True),
                    deps=[ev_vv_ops[seq], qkm_ch[s][j]] + bdeps
                    + ([last_wp_mm] if j == 0 else []))
                attn_ch[s][j] = amm
                aev = plan.add("S", lambda e, j=j, bank=bank: e.activation(
                    out=attn_sb[:, j * SUP:(j + 1) * SUP], in_=bank[:, :],
                    func=AF.Copy), deps=[amm])
                pb_ring.done(bidx, aev)
                ev_ats.append(aev)

            epi2 = []
            for a in range(NSUB):
                tile, tdeps, tok = pp2.acquire_full()
                mm = plan.add("T", lambda e, a=a, tile=tile: [
                    e.matmul(out=tile[:, oh * 512:(oh + 1) * 512],
                             lhsT=attn_sb[:, j * SUP + a * 128:j * SUP + (a + 1) * 128],
                             rhs=wp_sb[:, j * D + oh * 512:j * D + oh * 512 + 512],
                             start=(j == 0), stop=False)
                    for oh in range(2) for j in range(NCH)
                ] + [
                    e.matmul(out=tile[:, oh * 512:(oh + 1) * 512], lhsT=onesm[:, :],
                             rhs=bp_rep[:, oh * 512:(oh + 1) * 512], start=False, stop=True)
                    for oh in range(2)
                ], deps=ev_ats + tdeps + [wp_ready])
                last_wp_mm = mm
                epi = plan.add("V", lambda e, a=a, tile=tile, x1=x1: [
                    e.tensor_tensor(out=x1[:, a * D + oh * 512:a * D + (oh + 1) * 512],
                                    in0=tile[:, oh * 512:(oh + 1) * 512],
                                    in1=x1[:, a * D + oh * 512:a * D + (oh + 1) * 512],
                                    op=ALU.add)
                    for oh in range(2)
                ], deps=[mm, ldx1_by_sup[s]])
                pp2.done(tok, epi)
                epi2.append(epi)
                sta3 = plan.add("V", ln3[s % 2][3](x1, a), deps=[epi])
                cmb3 = plan.add("V", ln3[s % 2][5](a), deps=[sta3])
                st3s.append(plan.add("Q", ln3[s % 2][6](a), deps=[cmb3], sem="G"))
            epi2_by_sup[s] = epi2
            st3s_by_sup[s] = st3s

        def emit_2b_back(s):
            nonlocal last_tr3, last_ffn2_mm
            x1 = x1b[s % 2]
            x13 = x1[:, :].rearrange("p (a d) -> p a d", a=NSUB)
            epo = []
            mms3 = []
            for a in range(NSUB):
                fin3v = plan.add("V", ln3[s % 2][8](a), deps=[st3s_by_sup[s][a]])
                fin3 = plan.add("S", ln3[s % 2][9](x1, ybuf2, a),
                                deps=[fin3v,
                                      last_tr3 if a == 0 else None])
                bank, bdeps, bidx = pb_ring.acquire()
                bankb = bank[:, :].bitcast(BF16)
                tr = plan.add("T", lambda e, a=a, bankb=bankb: [
                    e.transpose(out=bankb[:, i * 128:(i + 1) * 128],
                                in_=ybuf2[:, a * D + i * 128:a * D + (i + 1) * 128],
                                identity=id_b[:, :])
                    for i in range(NCH)
                ], deps=[fin3] + bdeps)
                ev = plan.add("V", lambda e, a=a, bankb=bankb: e.tensor_copy(
                    out=y3T[:, :].rearrange("p (i t) -> p i t", i=NCH)[
                        :, :, (a % 2) * 128:(a % 2) * 128 + 128],
                    in_=bankb[:, :].rearrange("p (i t) -> p i t", i=NCH)),
                    deps=[tr] + ([mms3[a - 2]] if a >= 2 else []))
                pb_ring.done(bidx, ev)
                if a == 3:
                    last_tr3 = ev
                tile, tdeps, tok = pp2.acquire_full()
                mm = plan.add("T", lambda e, a=a, tile=tile: [
                    e.matmul(out=tile[:, oh * 512:(oh + 1) * 512],
                             lhsT=y3T[:, i * 256 + (a % 2) * 128:
                                      i * 256 + (a % 2) * 128 + 128],
                             rhs=w_half[:, i * D + oh * 512:i * D + oh * 512 + 512],
                             start=(i == 0), stop=False)
                    for oh in range(2) for i in range(NCH)
                ] + [
                    e.matmul(out=tile[:, oh * 512:(oh + 1) * 512], lhsT=onesm[:, :],
                             rhs=b_rep[:, oh * 512:(oh + 1) * 512], start=False, stop=True)
                    for oh in range(2)
                ], deps=[ev] + tdeps)
                mms3.append(mm)
                last_ffn2_mm = mm
                epi = plan.add("V", lambda e, a=a, tile=tile, x1=x1: [
                    e.tensor_tensor(out=x1[:, a * D + oh * 512:a * D + (oh + 1) * 512],
                                    in0=tile[:, oh * 512:(oh + 1) * 512],
                                    in1=x1[:, a * D + oh * 512:a * D + (oh + 1) * 512],
                                    op=ALU.add)
                    for oh in range(2)
                ], deps=[mm])
                pp2.done(tok, epi)
                epo.append(epi)

            st_a = plan.add("Y", lambda e, s=s, x13=x13: e.dma_start(
                out=out_ch[:, s * NSUB:s * NSUB + 2, :], in_=x13[:, 0:2, :]),
                deps=epo[0:2], ndma=1)
            st_b = plan.add("Q", lambda e, s=s, x13=x13: e.dma_start(
                out=out_ch[:, s * NSUB + 2:s * NSUB + 4, :], in_=x13[:, 2:4, :]),
                deps=epo[2:4], ndma=1)
            st_out_ops[s] = (st_a, st_b)

        wp_ready = w_ready["wp"]
        emit_2a_v(0)
        emit_2a_v(1)
        emit_2a_v(2)
        emit_2a_v(3)
        emit_fin_qk()
        emit_2a_qk(0)
        emit_2a_qk(1)
        emit_2b_front(0)
        emit_2a_v(4)
        emit_2a_qk(2)
        emit_2b_front(1)
        emit_2b_back(0)
        emit_2a_v(5)
        emit_2a_qk(3)
        emit_2b_front(2)
        emit_2b_back(1)
        emit_2a_v(6)
        emit_2a_qk(4)
        emit_2b_front(3)
        emit_2b_back(2)
        emit_2a_v(7)
        emit_2a_qk(5)
        pb_ring.slots.extend(vvps)
        pb_ring.last.extend([ev_vv_ops[0], ev_vv_ops[1]])
        emit_2b_front(4)
        emit_2b_back(3)
        emit_2a_qk(6)
        emit_2b_front(5)
        emit_2b_back(4)
        emit_2a_qk(7)
        emit_2b_front(6)
        emit_2b_back(5)
        emit_2b_front(7)
        emit_2b_back(6)
        emit_2b_back(7)

        plan.add("Q", lambda e: e.nop(),
                 deps=list(st_out_ops[-1]) + list(st_out_ops[-2]), sem="G")

        ps2a.close()
        ps2b.close()
        ph2.close()

        plan.finalize()
        with contextlib.ExitStack() as es2:
            sems = {n: es2.enter_context(nc.semaphore(f"sem{n}"))
                    for n in ("T", "V", "S", "Y", "Q", "G", "Z", "W",
                              "Cv", "Cq", "cT", "cV", "cS")}
            block = es2.enter_context(nc.Block())

            @block.sync
            def _(eng):
                plan.run(eng, "Y", sems)

            @block.tensor
            def _(eng):
                plan.run(eng, "T", sems)

            @block.vector
            def _(eng):
                plan.run(eng, "V", sems)

            @block.scalar
            def _(eng):
                plan.run(eng, "S", sems)

            @block.gpsimd
            def _(eng):
                plan.run(eng, "Q", sems)

    nc.reset()
    return nc


_prog_cache = {}


def _get_program(debug=DEBUG_OUTS):
    key = bool(debug)
    if key not in _prog_cache:
        _prog_cache[key] = build_program(debug)
    return _prog_cache[key]


def _prepare(L0_attn_x, L0_attn_Wq, L0_attn_gq, L0_attn_bq, L0_attn_Wk, L0_attn_gk,
             L0_attn_bk, L0_attn_Wv, L0_attn_gv, L0_attn_bv, L0_attn_Wp, L0_attn_bp,
             L0_ffn_W, L0_ffn_b, L0_ffn_lora_down, L0_ffn_lora_up, L0_ffn_down_aux,
             L0_ffn_up_aux):
    f32 = np.float32
    x = np.ascontiguousarray(np.asarray(L0_attn_x, dtype=f32))
    wqT = np.ascontiguousarray(np.asarray(L0_attn_Wq, dtype=f32).T)
    wkT = np.ascontiguousarray(np.asarray(L0_attn_Wk, dtype=f32).T)
    wvT = np.ascontiguousarray(np.asarray(L0_attn_Wv, dtype=f32).T)
    wpT = np.ascontiguousarray(np.asarray(L0_attn_Wp, dtype=f32).T)
    wfT = np.ascontiguousarray(np.asarray(L0_ffn_W, dtype=f32).T)
    lora_up = np.ascontiguousarray(np.asarray(L0_ffn_lora_up, dtype=f32))
    up_auxT = np.ascontiguousarray(np.asarray(L0_ffn_up_aux, dtype=f32).T)
    down_aux = np.ascontiguousarray(np.asarray(L0_ffn_down_aux, dtype=f32))
    lora_downT = np.ascontiguousarray(np.asarray(L0_ffn_lora_down, dtype=f32).T)
    col = lambda v: np.ascontiguousarray(np.asarray(v, dtype=f32).reshape(8, 128).T)
    gb16 = np.ascontiguousarray(np.concatenate(
        [col(L0_attn_gq), col(L0_attn_gk), col(L0_attn_bq), col(L0_attn_bk),
         col(L0_attn_gv), col(L0_attn_bv)], axis=1))
    bcols = np.ascontiguousarray(np.concatenate(
        [col(L0_ffn_b), col(L0_attn_bp)], axis=1))
    eye = np.eye(128, dtype=f32)

    shared = dict(wqT=wqT, wkT=wkT, wvT=wvT, wpT=wpT, wfT=wfT, lora_up=lora_up,
                  up_auxT=up_auxT, down_aux=down_aux, lora_downT=lora_downT,
                  gb16=gb16, bcols=bcols, eye=eye)
    in_maps = []
    for c in range(NCORE):
        m = dict(shared)
        m["x"] = np.ascontiguousarray(x[2 * c:2 * c + 2].reshape(T_CORE, D))
        in_maps.append(m)
    return in_maps


def kernel(**inputs):
    in_maps = _prepare(**inputs)
    nc = _get_program()
    res = run_bass_kernel_spmd(nc, in_maps, core_ids=list(range(NCORE)))
    out = np.concatenate([np.asarray(r["out"]).reshape(2, 2048, D)
                          for r in res.results], axis=0)
    return out.astype(np.float32)


def kernel_debug(**inputs):
    in_maps = _prepare(**inputs)
    nc = _get_program(debug=True)
    res = run_bass_kernel_spmd(nc, in_maps, core_ids=list(range(NCORE)))
    shards = [np.asarray(r["out"]) for r in res.results]
    return shards, res.results


# version 26 (final)
# speedup vs baseline: 1.0504x; 1.0504x over previous
import contextlib

import numpy as np

import concourse.bass as bass
import concourse.mybir as mybir
from concourse.bass_utils import run_bass_kernel_spmd

F32 = mybir.dt.float32
BF16 = mybir.dt.bfloat16
AF = mybir.ActivationFunctionType
ALU = mybir.AluOpType
AX = mybir.AxisListType

NCORE = 8
T_CORE = 4096
D = 1024
NCH = 8
NSUP = 8
SUP = 512
NSUB = 4
NTOK_GLOBAL = 32768.0
LN_EPS = 1e-5
BN_EPS = 1e-5

DEBUG_OUTS = False


class _Op:
    __slots__ = ("stream", "sem", "fn", "deps", "ndma", "inc", "cum", "idx",
                 "dep_target")

    def __init__(self, stream, sem, fn, deps, ndma):
        self.stream, self.sem, self.fn = stream, sem, fn
        self.deps, self.ndma = deps, ndma
        self.inc = 0
        self.cum = None


class Plan:

    def __init__(self):
        self.ops = []
        self.last_by_stream = {}
        self._eng = None
        self._csem = None
        self._ccnt = None
        self._barrier_after = None

    def add(self, stream, fn, deps=(), ndma=0, sem=None):
        deps = [d for d in deps if d is not None]
        op = _Op(stream, sem or stream, fn, deps, ndma)
        op.idx = len(self.ops)
        self.ops.append(op)
        self.last_by_stream[stream] = op
        return op

    def barrier(self):
        lasts = dict(self.last_by_stream)
        outs = {}
        for st in ("T", "V", "S", "Y", "Q"):
            deps = [op for s2, op in lasts.items() if s2 != st]
            outs[st] = self.add(st, lambda eng: eng.nop(), deps=deps)
        return outs

    def finalize(self):
        needed = set()
        for op in self.ops:
            for d in op.deps:
                needed.add(id(d))
        cum = {}
        for op in self.ops:
            op.dep_target = id(op) in needed
            if op.ndma:
                amt = 16 * op.ndma
            elif op.dep_target:
                amt = 1
            else:
                amt = 0
            op.inc = amt
            if amt:
                cum[op.sem] = cum.get(op.sem, 0) + amt
                op.cum = cum[op.sem]

    def B(self):
        inst = self._proxy.last
        assert inst is not None, "B() with no prior instruction"
        self._ccnt[0] += 1
        inst.then_inc(self._csem, 1)
        self._eng.wait_ge(self._csem, self._ccnt[0])
        self._proxy.last = None

    def run(self, eng, stream, sems):
        waited = {}
        self._eng = eng
        self._csem = sems.get("c" + stream)
        self._ccnt = [0]
        self._proxy = _EngRecorder(eng)
        for op in self.ops:
            if op.stream != stream:
                continue
            for d in op.deps:
                if d.stream == stream and stream == "Y":
                    continue
                assert d.inc > 0, "dep target lacks inc"
                if waited.get(d.sem, -1) < d.cum:
                    eng.wait_ge(sems[d.sem], d.cum)
                    waited[d.sem] = d.cum
            ret = op.fn(self._proxy)
            if op.inc:
                if op.ndma:
                    insts = ret if isinstance(ret, (list, tuple)) else [ret]
                    assert len(insts) == op.ndma, "ndma mismatch"
                    for inst in insts:
                        inst.then_inc(sems[op.sem], 16)
                    if op.dep_target:
                        eng.wait_ge(sems[op.sem], op.cum)
                        waited[op.sem] = op.cum
                else:
                    inst = ret[-1] if isinstance(ret, (list, tuple)) else ret
                    assert inst is not None, "op fn returned nothing"
                    inst.then_inc(sems[op.sem], 1)


class _EngRecorder:

    def __init__(self, eng):
        self._eng = eng
        self.last = None

    def __getattr__(self, name):
        f = getattr(self._eng, name)
        if not callable(f):
            return f
        rec = self

        def wrap(*a, **kw):
            r = f(*a, **kw)
            if hasattr(r, "then_inc"):
                rec.last = r
            return r

        return wrap


class PsumRing:

    def __init__(self, tiles):
        self.halves = []
        for t in tiles:
            self.halves.append(t[:, 0:512])
            self.halves.append(t[:, 512:1024])
        self.tiles = tiles
        self.last = [None] * len(self.halves)
        self.i = 0

    def acquire_half(self):
        idx = self.i % len(self.halves)
        self.i += 1
        return self.halves[idx], [self.last[idx]], (idx,)

    def acquire_full(self):
        if self.i % 2:
            self.i += 1
        idx = self.i % len(self.halves)
        self.i += 2
        tile = self.tiles[idx // 2]
        return tile, [self.last[idx], self.last[idx + 1]], (idx, idx + 1)

    def done(self, token, op):
        for idx in token:
            self.last[idx] = op


class Ring2:

    def __init__(self, slots):
        self.slots = slots
        self.last = [None] * len(slots)
        self.i = 0

    def acquire(self):
        idx = self.i % len(self.slots)
        self.i += 1
        return self.slots[idx], [self.last[idx]], idx

    def done(self, idx, op):
        self.last[idx] = op


def build_program(debug=DEBUG_OUTS):
    nc = bass.Bass("TRN2", target_bir_lowering=False, detect_race_conditions=True)

    dt_in = lambda n, s: nc.dram_tensor(n, s, F32, kind="ExternalInput").ap()

    x_in = dt_in("x", [T_CORE, D])
    wqT = dt_in("wqT", [D, D])
    wkT = dt_in("wkT", [D, D])
    wvT = dt_in("wvT", [D, D])
    wpT = dt_in("wpT", [D, D])
    wfT = dt_in("wfT", [D, D])
    lora_up = dt_in("lora_up", [50, 4])
    up_auxT = dt_in("up_auxT", [50, D])
    down_aux = dt_in("down_aux", [100, D])
    lora_downT = dt_in("lora_downT", [100, 4])
    gb16 = dt_in("gb16", [128, 48])
    bcols = dt_in("bcols", [128, 16])
    eye_in = dt_in("eye", [128, 128])

    out_d = nc.dram_tensor("out", [T_CORE, D], F32, kind="ExternalOutput").ap()

    def scratch(name, shape, dtype):
        kind = "ExternalOutput" if debug else "Internal"
        return nc.dram_tensor(name, shape, dtype, kind=kind).ap()

    x1_d = scratch("x1_d", [NSUP, 128, NSUB * D], F32)
    hq_d = scratch("hq_d", [NSUP, 128, NCH * SUP], BF16)
    hk_d = scratch("hk_d", [NSUP, 128, NCH * SUP], BF16)
    hv_d = scratch("hv_d", [NSUP, 128, NCH * SUP], BF16)
    ccv_in = nc.dram_tensor("ccv_in", [2048], F32).ap()
    ccv_out = nc.dram_tensor("ccv_out", [2048], F32).ap()
    ccqk_in = nc.dram_tensor("ccqk_in", [4096], F32).ap()
    ccqk_out = nc.dram_tensor("ccqk_out", [4096], F32).ap()
    rowscr = nc.dram_tensor("rowscr", [2, D], F32).ap()

    plan = Plan()

    def chunked(dram_ap, p=128):
        return dram_ap.rearrange("(c p) t -> p c t", p=p)

    with contextlib.ExitStack() as es:
        E = es.enter_context

        w_half = E(nc.sbuf_tensor("w_half", [128, NCH * D], BF16))
        b_rep = E(nc.sbuf_tensor("b_rep", [128, D], BF16))
        bp_rep = E(nc.sbuf_tensor("bp_rep", [128, D], BF16))
        id_b = E(nc.sbuf_tensor("id_b", [128, 128], BF16))
        onesm = E(nc.sbuf_tensor("onesm", [128, 128], BF16))
        gbcols = E(nc.sbuf_tensor("gbcols", [128, 48], F32))
        alph = E(nc.sbuf_tensor("alph", [128, 16], F32))
        coff = E(nc.sbuf_tensor("coff", [128, 16], F32))
        av_c = E(nc.sbuf_tensor("av_c", [128, 8], F32))
        cv_c = E(nc.sbuf_tensor("cv_c", [128, 8], F32))
        neghalf = E(nc.sbuf_tensor("neghalf", [128, 16], F32))
        qacc = E(nc.sbuf_tensor("qacc", [128, 2 * 64], F32))
        kacc = E(nc.sbuf_tensor("kacc", [128, 2 * 64], F32))
        vacc = E(nc.sbuf_tensor("vacc", [128, 2 * 64], F32))
        ccv_sb = E(nc.sbuf_tensor("ccv_sb", [128, 16], F32))
        ccqk_sb = E(nc.sbuf_tensor("ccqk_sb", [128, 32], F32))
        st48v = E(nc.sbuf_tensor("st48v", [128, 16], F32))
        st48qk = E(nc.sbuf_tensor("st48qk", [128, 32], F32))
        VV = [E(nc.sbuf_tensor(f"VV{q}", [128, NCH * 128], BF16)) for q in range(2)]
        stt_sc = E(nc.sbuf_tensor("stt_sc", [128, 16], F32))
        stt2 = E(nc.sbuf_tensor("stt2", [128, 16], F32))
        wp_sb = E(nc.sbuf_tensor("wp_sb", [128, NCH * D], BF16))

        ph1 = contextlib.ExitStack()
        E1 = ph1.enter_context
        wq_sb = E1(nc.sbuf_tensor("wq_sb", [128, NCH * D], BF16))
        wk_sb = E1(nc.sbuf_tensor("wk_sb", [128, NCH * D], BF16))
        wv_sb = E1(nc.sbuf_tensor("wv_sb", [128, NCH * D], BF16))
        xbuf = [E1(nc.sbuf_tensor(f"xbuf{i}", [128, NSUB * D], F32)) for i in range(2)]
        ybuf = E1(nc.sbuf_tensor("ybuf", [128, NSUB * D], BF16))
        ybuf2p = E1(nc.sbuf_tensor("ybuf2p", [128, NSUB * D], BF16))
        y1T = E1(nc.sbuf_tensor("y1T", [128, NCH * SUP], BF16))
        y2T = E1(nc.sbuf_tensor("y2T", [128, NCH * SUP], BF16))
        hq_sb = E1(nc.sbuf_tensor("hq_sb", [128, NCH * SUP], BF16))
        hk_sb = E1(nc.sbuf_tensor("hk_sb", [128, NCH * SUP], BF16))
        hv_sb = E1(nc.sbuf_tensor("hv_sb", [128, NCH * SUP], BF16))
        junk = E1(nc.sbuf_tensor("junk", [128, 512], BF16))
        bc_sb = E1(nc.sbuf_tensor("bc_sb", [128, 16], F32))
        xps = E1(nc.sbuf_tensor("xps", [8, 256], F32))
        lnst = E1(nc.sbuf_tensor("lnst", [128, 48], F32))
        lnsm = E1(nc.sbuf_tensor("lnsm", [128, 10 * NSUB], F32))
        lnst_b = E1(nc.sbuf_tensor("lnst_b", [128, 48], F32))
        lnsm_b = E1(nc.sbuf_tensor("lnsm_b", [128, 10 * NSUB], F32))
        lnsc_a = E1(nc.sbuf_tensor("lnsc_a", [128, 16], F32))
        lnsc_b = E1(nc.sbuf_tensor("lnsc_b", [128, 16], F32))
        id_f = E1(nc.sbuf_tensor("id_f", [128, 128], F32))
        wstage = E1(nc.sbuf_tensor("wstage", [128, D], F32))
        wbig = [E1(nc.sbuf_tensor("wbig0", [128, 2 * D], F32))] * 2
        lup_sb = E1(nc.sbuf_tensor("lup_sb", [50, 4], F32))
        ldT_sb = E1(nc.sbuf_tensor("ldT_sb", [100, 4], F32))
        c_sb = E1(nc.sbuf_tensor("c_sb", [4, D], F32))
        a_sb = E1(nc.sbuf_tensor("a_sb", [128, 32], F32))
        at_sb = E1(nc.sbuf_tensor("at_sb", [4, D], F32))

        ps1 = contextlib.ExitStack()
        P1 = ps1.enter_context
        pp0 = P1(nc.psum_tensor("pp0", [128, D], F32))
        pp1 = P1(nc.psum_tensor("pp1", [128, D], F32))
        pp2x = P1(nc.psum_tensor("pp2x", [128, D], F32))
        trp = [P1(nc.psum_tensor(f"trp{i}", [128, 8 * 128], BF16)) for i in range(2)]

        pp = PsumRing([pp0, pp1, pp2x])
        trr = Ring2(trp)

        ld0 = plan.add("Y", lambda e: [
            e.dma_start(out=id_f[:, :], in_=eye_in),
            e.dma_start(out=gbcols[:, :], in_=gb16),
            e.dma_start(out=bc_sb[:, :], in_=bcols),
            e.dma_start(out=lup_sb[:, :], in_=lora_up),
            e.dma_start(out=wbig[0][0:50, 0:D], in_=up_auxT),
            e.dma_start(out=wbig[0][0:100, D:2 * D], in_=down_aux),
            e.dma_start(out=ldT_sb[:, :], in_=lora_downT),
        ], ndma=7)

        def consts_fn(e):
            e.tensor_copy(out=id_b[:, :], in_=id_f[:, :])
            e.memset(onesm[:, :], 1.0)
            e.memset(neghalf[:, :], -0.5)
            return e.tensor_scalar(out=bc_sb[:, 0:8], in0=bc_sb[:, 0:8], scalar1=0.5,
                                   scalar2=None, op0=ALU.mult)

        consts = plan.add("V", consts_fn, deps=[ld0])

        x_ch = chunked(x_in)
        ldx_ops = [None] * NSUP
        xb3_0 = xbuf[0][:, :].rearrange("p (a d) -> p a d", a=NSUB)
        ldx_ops[0] = plan.add("S", lambda e: e.dma_start(
            out=xb3_0, in_=x_ch[:, 0:NSUB, :]), ndma=1, sem="Z")


        mmC = plan.add("T", lambda e: [
            e.matmul(out=pp0[0:4, 0:512], lhsT=lup_sb[:, :], rhs=wbig[0][0:50, 0:512],
                     start=True, stop=True),
            e.matmul(out=pp0[0:4, 512:1024], lhsT=lup_sb[:, :],
                     rhs=wbig[0][0:50, 512:D], start=True, stop=True),
        ], deps=[ld0])
        evC = plan.add("V", lambda e: e.tensor_scalar(
            out=c_sb[:, :], in0=pp0[0:4, :], scalar1=0.5, scalar2=None, op0=ALU.mult),
            deps=[mmC])
        mmA = plan.add("T", lambda e: [
            e.matmul(out=pp1[:, i * 4:(i + 1) * 4],
                     lhsT=wbig[0][0:100, D + i * 128:D + (i + 1) * 128],
                     rhs=ldT_sb[:, :], start=True, stop=True, skip_group_check=True)
            for i in range(NCH)
        ], deps=[ld0])
        evA = plan.add("V", lambda e: e.tensor_copy(out=a_sb[:, :], in_=pp1[:, 0:32]),
                       deps=[mmA])
        mmAT = plan.add("T", lambda e: [
            e.transpose(out=pp0[0:4, i * 128:(i + 1) * 128],
                        in_=a_sb[:, i * 4:(i + 1) * 4], identity=id_f[:, :])
            for i in range(NCH)
        ], deps=[evA, evC])
        evAT = plan.add("V", lambda e: e.tensor_copy(out=at_sb[:, :], in_=pp0[0:4, :]),
                        deps=[mmAT])

        stg_bufs = [wbig[0][:, :], hq_sb[:, :].bitcast(F32),
                    hk_sb[:, :].bitcast(F32), hv_sb[:, :].bitcast(F32)]
        buf_last = [[mmC, mmA], [], [], []]

        def stage_q(w_dram, h, b, queue):
            deps = list(buf_last[b])
            op = plan.add(queue, lambda e, w_dram=w_dram, h=h, b=b: e.dma_start(
                out=stg_bufs[b].rearrange("p (c o) -> p c o", c=2),
                in_=w_dram[h * 256:(h + 1) * 256, :].rearrange(
                    "(c p) o -> p c o", p=128)),
                deps=deps, ndma=1, sem="Z" if queue == "S" else None)
            return op

        wf_loads = []
        for h in range(4):
            wf_loads.append(stage_q(wfT, h, h, "Y" if h % 2 == 0 else "Q"))

        bias_tr = plan.add("T", lambda e: [
            e.transpose(out=pp0[0:8, 0:128], in_=bc_sb[:, 0:8], identity=id_f[:, :]),
            e.transpose(out=pp0[0:8, 128:256], in_=bc_sb[:, 8:16], identity=id_f[:, :]),
        ], deps=[consts, evAT])
        bias_ev = plan.add("V", lambda e: e.tensor_copy(out=xps[:, :], in_=pp0[0:8, 0:256]),
                           deps=[bias_tr])
        rep_out = plan.add("Q", lambda e: [
            e.dma_start(out=rowscr[0:1, :].rearrange("a (c w) -> (a c) w", c=8),
                        in_=xps[:, 0:128]),
            e.dma_start(out=rowscr[1:2, :].rearrange("a (c w) -> (a c) w", c=8),
                        in_=xps[:, 128:256]),
        ], deps=[bias_ev], ndma=2)
        rep_back1 = plan.add("Q", lambda e: e.dma_start(
            out=wstage[:, :], in_=rowscr[0:1, :].partition_broadcast(128)),
            deps=[rep_out], ndma=1)
        rep_fin1 = plan.add("V", lambda e: e.tensor_scalar(
            out=b_rep[:, :], in0=wstage[:, :], scalar1=1.0 / 128,
            scalar2=None, op0=ALU.mult), deps=[rep_back1])
        rep_back2 = plan.add("Q", lambda e: e.dma_start(
            out=wstage[:, :], in_=rowscr[1:2, :].partition_broadcast(128)),
            deps=[rep_fin1], ndma=1)
        rep_fin = plan.add("V", lambda e: e.tensor_scalar(
            out=bp_rep[:, :], in0=wstage[:, :], scalar1=1.0 / 128,
            scalar2=None, op0=ALU.mult), deps=[rep_back2])

        xb3_1 = xbuf[1][:, :].rearrange("p (a d) -> p a d", a=NSUB)
        ldx_ops[1] = plan.add("Q", lambda e: e.dma_start(
            out=xb3_1, in_=x_ch[:, NSUB:2 * NSUB, :]), ndma=1)

        stt_hist = []
        for h in range(4):
            for i in range(h * 2, h * 2 + 2):
                tile = pp1 if i % 2 else pp0
                src = stg_bufs[h][:, (i % 2) * D:(i % 2 + 1) * D]
                mmL = plan.add("T", lambda e, i=i, tile=tile: [
                    e.matmul(out=tile[:, 0:512], lhsT=at_sb[:, i * 128:(i + 1) * 128],
                             rhs=c_sb[:, 0:512], start=True, stop=True,
                             skip_group_check=True),
                    e.matmul(out=tile[:, 512:1024], lhsT=at_sb[:, i * 128:(i + 1) * 128],
                             rhs=c_sb[:, 512:1024], start=True, stop=True,
                             skip_group_check=True),
                ], deps=[evAT]
                    + ([bias_ev] if i == 0 else [])
                    + ([evA] if i == 1 else [])
                    + ([stt_hist[i - 2]] if i >= 2 else []))
                stt = plan.add("V", lambda e, src=src, i=i, tile=tile:
                               e.scalar_tensor_tensor(
                                   out=w_half[:, i * D:(i + 1) * D], in0=src,
                                   scalar=0.5, in1=tile[:, :], op0=ALU.mult,
                                   op1=ALU.add),
                               deps=[wf_loads[h], mmL])
                stt_hist.append(stt)
            buf_last[h] = [stt_hist[-1]]
        whalf_ready = stt_hist[-1]

        w_ready = {}
        qk_loads = {}
        for wname, w_dram in (("q", wqT), ("k", wkT)):
            qk_loads[wname] = [
                stage_q(w_dram, h, h, "Y" if h % 2 == 0 else "S")
                for h in range(4)]

        def emit_cast(wname, w_sb, h, b, ld):
            cast = plan.add("V", lambda e, h=h, b=b, w_sb=w_sb: e.tensor_copy(
                out=w_sb[:, h * 2 * D:(h + 1) * 2 * D],
                in_=stg_bufs[b][:, :]), deps=[ld])
            buf_last[b] = [cast]
            w_ready[wname] = cast
            return cast

        for h in range(4):
            emit_cast("q", wq_sb, h, h, qk_loads["q"][h])
        for h in range(4):
            qk_loads["k"][h].deps.append(buf_last[h][0])

        stg_guards = {}
        vbufs = (0, 3, 0, 3)

        def emit_k_casts_and_v_loads():
            for h in range(4):
                emit_cast("k", wk_sb, h, h, qk_loads["k"][h])
            stg_guards["q"] = buf_last[1][0]
            qk_loads["v"] = [
                stage_q(wvT, h, vbufs[h], "Y" if h % 2 == 0 else "Q")
                for h in range(4)]
            for h in range(4):
                cast = emit_cast("v", wv_sb, h, vbufs[h], qk_loads["v"][h])
                if h + 2 < 4:
                    qk_loads["v"][h + 2].deps.append(cast)

        def emit_wp_staging():
            for h in range(4):
                ld = stage_q(wpT, h, 0, "Y")
                emit_cast("wp", wp_sb, h, 0, ld)

        pp.last = [stt_hist[6], stt_hist[6], stt_hist[7], stt_hist[7],
                   None, None]

        self_B = plan.B

        def make_ln(lnst_t, lnsm_t, sc_t):
            msum = lnsm_t[:, 0:4]
            m2sum = lnsm_t[:, 4:8]
            msqs = lnsm_t[:, 8:12]
            mean4 = lnsm_t[:, 12:16]
            vare = lnsm_t[:, 16:20]
            mean2 = lnsm_t[:, 20:24]
            rs4 = lnsm_t[:, 32:36]
            nmrs = lnsm_t[:, 36:40]
            st3 = lnst_t[:, :].rearrange("p (a g k) -> p a g k", a=NSUB, k=3)

            def stats_sub_fn(xb, a):
                def fn(e):
                    e.bn_stats(out=lnst_t[:, a * 12:a * 12 + 6],
                               in_=xb[:, a * D:a * D + 512])
                    return e.bn_stats(out=lnst_t[:, a * 12 + 6:a * 12 + 12],
                                      in_=xb[:, a * D + 512:(a + 1) * D])

                return fn

            def comb_fn():
                def fn(e):
                    for a in range(NSUB):
                        means = st3[:, a, :, 1]
                        m2s = st3[:, a, :, 2]
                        e.reduce_sum(out=msum[:, a:a + 1], in_=means, axis=AX.X)
                        e.reduce_sum(out=m2sum[:, a:a + 1], in_=m2s, axis=AX.X)
                        e.scalar_tensor_tensor(
                            out=sc_t[:, a * 4:(a + 1) * 4], in0=means, scalar=1.0,
                            in1=means, op0=ALU.mult, op1=ALU.mult,
                            accum_out=msqs[:, a:a + 1])
                    self_B()
                    e.tensor_scalar(out=mean4, in0=msum, scalar1=0.25,
                                    scalar2=None, op0=ALU.mult)
                    e.tensor_scalar(out=vare, in0=m2sum, scalar1=1.0 / 1024,
                                    scalar2=LN_EPS, op0=ALU.mult, op1=ALU.add)
                    self_B()
                    e.tensor_tensor(out=mean2, in0=mean4, in1=mean4, op=ALU.mult)
                    e.scalar_tensor_tensor(out=vare, in0=msqs, scalar=0.25, in1=vare,
                                           op0=ALU.mult, op1=ALU.add)
                    self_B()
                    return e.tensor_tensor(out=vare, in0=vare, in1=mean2,
                                           op=ALU.subtract)

                return fn

            def stats_fn(xb):
                subs = [stats_sub_fn(xb, a) for a in range(NSUB)]
                comb = comb_fn()

                def fn(e):
                    for sf in subs:
                        sf(e)
                    self_B()
                    return comb(e)

                return fn

            def comb_sub_fn(a):
                def fn(e):
                    means = st3[:, a, :, 1]
                    m2s = st3[:, a, :, 2]
                    e.reduce_sum(out=msum[:, a:a + 1], in_=means, axis=AX.X)
                    e.reduce_sum(out=m2sum[:, a:a + 1], in_=m2s, axis=AX.X)
                    e.scalar_tensor_tensor(
                        out=sc_t[:, a * 4:(a + 1) * 4], in0=means, scalar=1.0,
                        in1=means, op0=ALU.mult, op1=ALU.mult,
                        accum_out=msqs[:, a:a + 1])
                    self_B()
                    e.tensor_scalar(out=mean4[:, a:a + 1], in0=msum[:, a:a + 1],
                                    scalar1=0.25, scalar2=None, op0=ALU.mult)
                    e.tensor_scalar(out=vare[:, a:a + 1], in0=m2sum[:, a:a + 1],
                                    scalar1=1.0 / 1024, scalar2=LN_EPS,
                                    op0=ALU.mult, op1=ALU.add)
                    self_B()
                    e.tensor_tensor(out=mean2[:, a:a + 1], in0=mean4[:, a:a + 1],
                                    in1=mean4[:, a:a + 1], op=ALU.mult)
                    e.scalar_tensor_tensor(out=vare[:, a:a + 1], in0=msqs[:, a:a + 1],
                                           scalar=0.25, in1=vare[:, a:a + 1],
                                           op0=ALU.mult, op1=ALU.add)
                    self_B()
                    return e.tensor_tensor(out=vare[:, a:a + 1], in0=vare[:, a:a + 1],
                                           in1=mean2[:, a:a + 1], op=ALU.subtract)

                return fn

            def rs_sub_fn(a):
                def fn(e):
                    return e.tensor_tensor(out=rs4[:, a:a + 1], in0=vare[:, a:a + 1],
                                           in1=neghalf[:, 0:1], op=ALU.pow)

                return fn

            def fin_sub_fn(xb, ydst, a):
                def fn(e):
                    e.scalar_tensor_tensor(out=nmrs[:, a:a + 1], in0=mean4[:, a:a + 1],
                                           scalar=-1.0, in1=rs4[:, a:a + 1],
                                           op0=ALU.mult, op1=ALU.mult)
                    self_B()
                    return e.tensor_scalar(
                        out=ydst[:, a * D:(a + 1) * D], in0=xb[:, a * D:(a + 1) * D],
                        scalar1=rs4[:, a:a + 1], scalar2=nmrs[:, a:a + 1],
                        op0=ALU.mult, op1=ALU.add)

                return fn

            def nmrs_sub_fn(a):
                def fn(e):
                    return e.scalar_tensor_tensor(
                        out=nmrs[:, a:a + 1], in0=mean4[:, a:a + 1], scalar=-1.0,
                        in1=rs4[:, a:a + 1], op0=ALU.mult, op1=ALU.mult)

                return fn

            def apply_sub_fn(xb, ydst, a):
                def fn(e):
                    return e.activation(
                        out=ydst[:, a * D:(a + 1) * D],
                        in_=xb[:, a * D:(a + 1) * D], func=AF.Identity,
                        scale=rs4[:, a:a + 1], bias=nmrs[:, a:a + 1])

                return fn

            def rs_fn(e):
                return e.tensor_tensor(out=rs4, in0=vare, in1=neghalf[:, 0:4],
                                       op=ALU.pow)

            def fin_fn(xb, ydst):
                def fn(e):
                    e.scalar_tensor_tensor(out=nmrs, in0=mean4, scalar=-1.0, in1=rs4,
                                           op0=ALU.mult, op1=ALU.mult)
                    self_B()
                    insts = []
                    for a in range(NSUB):
                        insts.append(e.tensor_scalar(
                            out=ydst[:, a * D:(a + 1) * D], in0=xb[:, a * D:(a + 1) * D],
                            scalar1=rs4[:, a:a + 1], scalar2=nmrs[:, a:a + 1],
                            op0=ALU.mult, op1=ALU.add))
                    return insts

                return fn

            return (stats_fn, rs_fn, fin_fn, stats_sub_fn, comb_fn,
                    comb_sub_fn, rs_sub_fn, fin_sub_fn, nmrs_sub_fn,
                    apply_sub_fn)

        (ln_stats_fn, ln_rs_fn, ln_fin_fn, ln_stats_sub_fn, ln_comb_fn,
         _, _, _, _, _) = make_ln(lnst, lnsm, lnsc_a)
        (_, _, _, ln2_stats_sub_fn, _, ln2_comb_sub_fn, ln2_rs_sub_fn,
         ln2_fin_sub_fn, _, _) = make_ln(lnst_b, lnsm_b, lnsc_b)

        def emit_transposes(ydst, yTdst, dep_fin, dep_reuse):
            evs = []
            for a in range(NSUB):
                bank, bdeps, bidx = trr.acquire()
                dfin = dep_fin[a] if isinstance(dep_fin, list) else dep_fin
                tr = plan.add("T", lambda e, a=a, bank=bank, ydst=ydst: [
                    e.transpose(out=bank[:, i * 128:(i + 1) * 128],
                                in_=ydst[:, a * D + i * 128:a * D + (i + 1) * 128],
                                identity=id_b[:, :])
                    for i in range(NCH)
                ], deps=[dfin] + bdeps + ([dep_reuse] if a == 0 else []))
                ev = plan.add("V", lambda e, a=a, bank=bank, yTdst=yTdst: e.tensor_copy(
                    out=yTdst[:, :].rearrange("p (i t) -> p i t", i=NCH)[:, :, a * 128:(a + 1) * 128],
                    in_=bank[:, :].rearrange("p (i t) -> p i t", i=NCH)),
                    deps=[tr])
                trr.done(bidx, ev)
                evs.append(ev)
            return evs

        fin2_ops = [None] * NSUP
        spx1_ops = [None] * NSUP
        sp_hq = [None] * NSUP
        sp_hk = [None] * NSUP
        sp_hv = [None] * NSUP
        last_ffn_mm = None
        last_brv_mm = None
        last_tr2 = None
        last_evq = {"q": None, "k": None}
        last_vst2 = None
        last_vst1 = None
        last_evq_all = []
        last_evk_all = []
        last_evv_all = []
        last_junk = [None]
        out_ch = chunked(out_d)

        fin1_ops = [None] * NSUP
        ev1_ops = [None] * NSUP

        rs1_ops = [None] * NSUP

        def emit_head_stats(s):
            if rs1_ops[s] is not None:
                return
            xb = xbuf[s % 2]
            xb3 = xb[:, :].rearrange("p (a d) -> p a d", a=NSUB)
            if ldx_ops[s] is None:
                ldx_ops[s] = plan.add("Y", lambda e, s=s, xb3=xb3: e.dma_start(
                    out=xb3, in_=x_ch[:, s * NSUB:(s + 1) * NSUB, :]),
                    deps=[fin2_ops[s - 2] if s >= 2 else None,
                          spx1_ops[s - 2] if s >= 2 else None],
                    ndma=1)
            st1 = plan.add("V", ln_stats_fn(xb), deps=[ldx_ops[s]])
            rs1_ops[s] = plan.add("Q", ln_rs_fn,
                                  deps=[st1, consts if s == 0 else None], sem="G")

        def emit_head_fin(s):
            xb = xbuf[s % 2]
            fin1_ops[s] = plan.add("V", ln_fin_fn(xb, ybuf),
                                   deps=[rs1_ops[s],
                                         ev1_ops[s - 1][3] if s >= 1 else None])

        def emit_head_tr(s):
            ev1_ops[s] = emit_transposes(ybuf, y1T, fin1_ops[s], last_ffn_mm)

        emit_head_stats(0)
        emit_head_fin(0)
        emit_head_tr(0)
        ffn_mm_ops = [[] for _ in range(NSUP)]

        def emit_ffn_mm_one(s, a):
            ev1 = ev1_ops[s]
            tile, tdeps, tok = pp.acquire_full()
            mm = plan.add("T", lambda e, a=a, tile=tile: [
                e.matmul(out=tile[:, oh * 512:(oh + 1) * 512],
                         lhsT=y1T[:, i * SUP + a * 128:i * SUP + (a + 1) * 128],
                         rhs=w_half[:, i * D + oh * 512:i * D + oh * 512 + 512],
                         start=(i == 0), stop=False)
                for oh in range(2) for i in range(NCH)
            ] + [
                e.matmul(out=tile[:, oh * 512:(oh + 1) * 512], lhsT=onesm[:, :],
                         rhs=b_rep[:, oh * 512:(oh + 1) * 512], start=False, stop=True)
                for oh in range(2)
            ], deps=[ev1[a]] + tdeps
                + ([whalf_ready, rep_fin, consts] if (s == 0 and a == 0) else []))
            ffn_mm_ops[s].append((mm, tile, tok))
            return mm

        for a0 in range(3):
            last_ffn_mm = emit_ffn_mm_one(0, a0)
        ffn_cursor = [3]
        epi_ops = [[None] * NSUB for _ in range(NSUP)]

        def emit_epi(s, a, pre=False):
            mm, tile, tok = ffn_mm_ops[s][a]
            xb = xbuf[s % 2]
            epi = plan.add("V", lambda e, a=a, tile=tile, xb=xb: [
                e.tensor_tensor(out=xb[:, a * D + oh * 512:a * D + (oh + 1) * 512],
                                in0=tile[:, oh * 512:(oh + 1) * 512],
                                in1=xb[:, a * D + oh * 512:a * D + (oh + 1) * 512],
                                op=ALU.add)
                for oh in range(2)
            ], deps=[mm] + ([fin1_ops[s]] if pre else []))
            pp.done(tok, epi)
            epi_ops[s][a] = epi
            return epi

        ln2fin_ops = [[None] * NSUB for _ in range(NSUP)]

        def emit_ln2_chain(s, a, pre_tr2=None):
            xb = xbuf[s % 2]
            sta = plan.add("V", ln2_stats_sub_fn(xb, a), deps=[epi_ops[s][a]])
            cmb = plan.add("V", ln2_comb_sub_fn(a), deps=[sta])
            rsx = plan.add("Q", ln2_rs_sub_fn(a), deps=[cmb], sem="G")
            fin = plan.add("V", ln2_fin_sub_fn(xb, ybuf2p, a),
                           deps=[rsx, pre_tr2])
            ln2fin_ops[s][a] = fin
            return fin

        for s in range(NSUP):
            xb = xbuf[s % 2]
            xb3 = xb[:, :].rearrange("p (a d) -> p a d", a=NSUB)
            if s + 1 < NSUP:
                emit_head_stats(s + 1)

            for a in range(NSUB):
                if ln2fin_ops[s][a] is None:
                    if epi_ops[s][a] is None:
                        emit_epi(s, a)
                    emit_ln2_chain(s, a, pre_tr2=last_tr2 if a == 0 else None)
                p = ffn_cursor[0]
                if p < NSUP * NSUB:
                    last_ffn_mm = emit_ffn_mm_one(p // NSUB, p % NSUB)
                    ffn_cursor[0] = p + 1
                if a == 0 and s + 1 < NSUP:
                    if fin1_ops[s + 1] is None:
                        emit_head_fin(s + 1)
                    emit_head_tr(s + 1)

            if s + 1 < NSUP:
                for a_n in range(3):
                    emit_epi(s + 1, a_n, pre=True)

            spx1 = plan.add("Q", lambda e, s=s, xb=xb: e.dma_start(
                out=x1_d[s], in_=xb[:, :]),
                deps=epi_ops[s], ndma=1)
            spx1_ops[s] = spx1

            fin2_ops[s] = ln2fin_ops[s][3]
            ev2 = emit_transposes(ybuf2p, y2T, ln2fin_ops[s], last_brv_mm)
            last_tr2 = ev2[3]

            if s == 0:
                emit_k_casts_and_v_loads()
            if s == 1:
                emit_wp_staging()

            if s + 2 < NSUP:
                emit_head_stats(s + 2)
            if s + 1 < NSUP:
                for a_n in range(3):
                    emit_ln2_chain(s + 1, a_n,
                                   pre_tr2=last_tr2 if a_n == 0 else None)
                if s + 2 < NSUP:
                    emit_head_fin(s + 2)

            br_last = {}
            for br, w_sb, h_sb, acc, sp_list, ev_all in (
                ("v", wv_sb, hv_sb, vacc, sp_hv, last_evv_all),
                ("q", wq_sb, hq_sb, qacc, sp_hq, last_evq_all),
                ("k", wk_sb, hk_sb, kacc, sp_hk, last_evk_all),
            ):
                evs = []
                for o in range(NCH):
                    half, hdeps, htok = pp.acquire_half()
                    mm = plan.add("T", lambda e, o=o, half=half, w_sb=w_sb: [
                        e.matmul(out=half,
                                 lhsT=w_sb[:, i * D + o * 128:i * D + (o + 1) * 128],
                                 rhs=y2T[:, i * SUP:(i + 1) * SUP],
                                 start=(i == 0), stop=(i == NCH - 1))
                        for i in range(NCH)
                    ], deps=[ev2[3]] + hdeps
                        + ([sp_list[s - 1]] if (o == 0 and s >= 1) else [])
                        + ([w_ready[br]] if (o == 0 and s == 0) else []))
                    ev = plan.add("S", lambda e, o=o, s=s, half=half, h_sb=h_sb, acc=acc: [
                        e.activation(out=h_sb[:, o * SUP:(o + 1) * SUP], in_=half,
                                     func=AF.Copy, accum_out=acc[:, o * 8 + s:o * 8 + s + 1]),
                        e.activation(out=junk[:, :], in_=half, func=AF.Square,
                                     accum_out=acc[:, 64 + o * 8 + s:64 + o * 8 + s + 1]),
                    ], deps=[mm, last_junk[0]]
                        + ([stg_guards["q"]] if (s == 0 and br == "q"
                                                 and o == 0) else []))
                    last_junk[0] = ev
                    pp.done(htok, ev)
                    evs.append(ev)
                ev_all.append(evs[-1])
                br_last[br] = evs[-1]
                if s == NSUP - 1 and br == "v":
                    red_v = plan.add("V", lambda e: [
                        e.reduce_sum(out=ccv_sb[:, g * 8 + o:g * 8 + o + 1],
                                     in_=vacc[:, :].rearrange(
                                         "p (g o s) -> p g o s",
                                         g=2, s=8)[:, g, o, :],
                                     axis=AX.X)
                        for g in range(2) for o in range(NCH)
                    ], deps=[br_last["v"]])
                    ccv_dma = plan.add("S", lambda e: e.dma_start(
                        out=ccv_in[:].rearrange("(p c) -> p c", c=16),
                        in_=ccv_sb[:, :]), deps=[red_v], ndma=1, sem="W")
                    coll_v = plan.add("Q", lambda e: e.collective_compute(
                        "AllReduce", ALU.add, replica_groups=[list(range(NCORE))],
                        ins=[ccv_in.opt()], outs=[ccv_out.opt()]),
                        deps=[ccv_dma], sem="Cv")
            if s == NSUP - 1:
                red_qk = plan.add("V", lambda e: [
                    e.reduce_sum(out=ccqk_sb[:, g * 16 + b * 8 + o:
                                             g * 16 + b * 8 + o + 1],
                                 in_=acc[:, :].rearrange("p (g o s) -> p g o s",
                                                         g=2, s=8)[:, g, o, :],
                                 axis=AX.X)
                    for g in range(2) for b, acc in ((0, qacc), (1, kacc))
                    for o in range(NCH)
                ], deps=[br_last["q"], br_last["k"]])
                ccqk_dma = plan.add("S", lambda e: e.dma_start(
                    out=ccqk_in[:].rearrange("(p c) -> p c", c=32),
                    in_=ccqk_sb[:, :]), deps=[red_qk], ndma=1, sem="W")
                coll_qk = plan.add("Q", lambda e: e.collective_compute(
                    "AllReduce", ALU.add, replica_groups=[list(range(NCORE))],
                    ins=[ccqk_in.opt()], outs=[ccqk_out.opt()]),
                    deps=[ccqk_dma], sem="Cq")

            sp = plan.add("Q", lambda e, s=s: [
                e.dma_start(out=hq_d[s], in_=hq_sb[:, :]),
                e.dma_start(out=hk_d[s], in_=hk_sb[:, :]),
                e.dma_start(out=hv_d[s], in_=hv_sb[:, :]),
            ], deps=[br_last["q"], br_last["k"], br_last["v"]], ndma=3)
            sp_hq[s] = sp_hk[s] = sp_hv[s] = sp

        ps1.close()
        ph1.close()
        plan.barrier()

        ccv_back = plan.add("Y", lambda e: e.dma_start(
            out=st48v[:, :], in_=ccv_out[:].rearrange("(p c) -> p c", c=16)),
            deps=[coll_v], ndma=1)

        S1 = stt2[:, 0:8]
        S2 = stt2[:, 8:16]

        def finv_a_fn(e):
            e.tensor_scalar(out=S1, in0=st48v[:, 0:8], scalar1=1.0 / NTOK_GLOBAL,
                            scalar2=None, op0=ALU.mult)
            e.tensor_scalar(out=cv_c[:, :], in0=st48v[:, 8:16],
                            scalar1=1.0 / NTOK_GLOBAL, scalar2=None, op0=ALU.mult)
            plan.B()
            e.tensor_tensor(out=S2, in0=S1, in1=S1, op=ALU.mult)
            plan.B()
            e.tensor_tensor(out=cv_c[:, :], in0=cv_c[:, :], in1=S2, op=ALU.subtract)
            plan.B()
            return e.tensor_scalar(out=cv_c[:, :], in0=cv_c[:, :], scalar1=BN_EPS,
                                   scalar2=None, op0=ALU.add)

        finv_a = plan.add("V", finv_a_fn, deps=[ccv_back])
        finv_b = plan.add("Q", lambda e: e.tensor_tensor(
            out=av_c[:, :], in0=cv_c[:, :], in1=neghalf[:, 0:8], op=ALU.pow),
            deps=[finv_a], sem="G")

        def finv_c_fn(e):
            e.scalar_tensor_tensor(out=av_c[:, :], in0=gbcols[:, 32:40], scalar=4.0,
                                   in1=av_c[:, :], op0=ALU.mult, op1=ALU.mult)
            plan.B()
            e.tensor_tensor(out=S1, in0=S1, in1=av_c[:, :], op=ALU.mult)
            plan.B()
            e.scalar_tensor_tensor(out=cv_c[:, :], in0=gbcols[:, 40:48], scalar=4.0,
                                   in1=S1, op0=ALU.mult, op1=ALU.subtract)
            plan.B()
            return e.tensor_scalar(out=cv_c[:, :], in0=cv_c[:, :], scalar1=-4.0,
                                   scalar2=None, op0=ALU.add)

        finv_c = plan.add("V", finv_c_fn, deps=[finv_b])

        mu16 = stt_sc[:, 0:16]

        def fin_a_fn(e):
            e.tensor_scalar(out=mu16, in0=st48qk[:, 0:16], scalar1=1.0 / NTOK_GLOBAL,
                            scalar2=None, op0=ALU.mult)
            e.tensor_scalar(out=coff[:, :], in0=st48qk[:, 16:32],
                            scalar1=1.0 / NTOK_GLOBAL, scalar2=None, op0=ALU.mult)
            plan.B()
            e.tensor_tensor(out=alph[:, :], in0=mu16, in1=mu16, op=ALU.mult)
            plan.B()
            e.tensor_tensor(out=coff[:, :], in0=coff[:, :], in1=alph[:, :],
                            op=ALU.subtract)
            plan.B()
            return e.tensor_scalar(out=coff[:, :], in0=coff[:, :], scalar1=BN_EPS,
                                   scalar2=None, op0=ALU.add)

        fin_c = None

        def emit_fin_qk():
            nonlocal fin_c
            ccqk_back = plan.add("Q", lambda e: e.dma_start(
                out=st48qk[:, :],
                in_=ccqk_out[:].rearrange("(p c) -> p c", c=32)),
                deps=[coll_qk], ndma=1)
            fin_a = plan.add("V", fin_a_fn, deps=[ccqk_back])
            fin_b = plan.add("Q", lambda e: e.tensor_tensor(
                out=alph[:, :], in0=coff[:, :], in1=neghalf[:, 0:16], op=ALU.pow),
                deps=[fin_a], sem="G")
            fin_c = plan.add("V", fin_c_fn, deps=[fin_b])

        def fin_c_fn(e):
            e.scalar_tensor_tensor(out=alph[:, :], in0=gbcols[:, 0:16], scalar=4.0,
                                   in1=alph[:, :], op0=ALU.mult, op1=ALU.mult)
            plan.B()
            e.tensor_tensor(out=mu16, in0=mu16, in1=alph[:, :], op=ALU.mult)
            plan.B()
            e.scalar_tensor_tensor(out=coff[:, :], in0=gbcols[:, 16:32], scalar=4.0,
                                   in1=mu16, op0=ALU.mult, op1=ALU.subtract)
            plan.B()
            return e.tensor_scalar(out=coff[:, :], in0=coff[:, :], scalar1=-4.0,
                                   scalar2=None, op0=ALU.add)

        ph2 = contextlib.ExitStack()
        E2 = ph2.enter_context
        hq_in = [E2(nc.sbuf_tensor(f"hq_in{i}", [128, NCH * SUP], BF16))
                 for i in range(2)]
        hk_in = [E2(nc.sbuf_tensor(f"hk_in{i}", [128, NCH * SUP], BF16))
                 for i in range(2)]
        hv_in = [E2(nc.sbuf_tensor(f"hv_in{i}", [128, NCH * SUP], BF16))
                 for i in range(2)]
        qbuf = [E2(nc.sbuf_tensor(f"qbuf{i}", [128, NCH * SUP], BF16))
                for i in range(2)]
        qk_ring = [E2(nc.sbuf_tensor(f"qk_r{i}", [128, NCH * SUP], BF16))
                   for i in range(3)]
        vfm_sb = [E2(nc.sbuf_tensor(f"vfm_sb{i}", [128, NCH * SUP], BF16))
                  for i in range(2)]
        v_tok = E2(nc.sbuf_tensor("v_tok", [128, NSUB * D], BF16))

        ps2b = contextlib.ExitStack()
        P2 = ps2b.enter_context
        at_ps = [P2(nc.psum_tensor(f"at_ps{i}", [128, 512], F32)) for i in range(2)]
        qp0 = P2(nc.psum_tensor("qp0", [128, D], F32))
        qp1 = P2(nc.psum_tensor("qp1", [128, D], F32))
        pp2 = PsumRing([qp0, qp1])

        ps2a = contextlib.ExitStack()
        vvps = [ps2a.enter_context(nc.psum_tensor(f"vvps{q}", [128, 512], F32))
                for q in range(2)]

        zeroVV = plan.add("V", lambda e: [
            e.memset(VV[0][:, :], 0.0),
            e.memset(VV[1][:, :], 0.0),
        ])

        x1b = [E2(nc.sbuf_tensor(f"x1b{i}", [128, NSUB * D], F32)) for i in range(2)]
        attn_sb = E2(nc.sbuf_tensor("attn_sb", [128, NCH * SUP], BF16))
        ybuf2 = E2(nc.sbuf_tensor("ybuf2", [128, NSUB * D], BF16))
        y3T = E2(nc.sbuf_tensor("y3T", [128, NCH * 256], BF16))
        ln2st = [E2(nc.sbuf_tensor(f"ln2st{i}", [128, 48], F32)) for i in range(2)]
        ln2sm = [E2(nc.sbuf_tensor(f"ln2sm{i}", [128, 10 * NSUB], F32))
                 for i in range(2)]
        ln3sc = [E2(nc.sbuf_tensor(f"ln3sc{i}", [128, 16], F32)) for i in range(2)]
        ln3 = [make_ln(ln2st[i], ln2sm[i], ln3sc[i]) for i in range(2)]

        qsig_ops = [None] * NSUP
        ksig_ops = [None] * NSUP
        qkm_ch = [None] * NSUP
        attn_ch = [None] * NSUP
        vsig_ops = [None] * NSUP
        vtr_ops = [None] * NSUP
        vvmm_ops = [None] * NSUP
        ev_vv_ops = [None, None]

        def emit_2a_v(s):
            seq = s // 4
            hv = hv_in[s % 2]
            vfm = vfm_sb[s % 2]
            vt = v_tok
            ldhv = plan.add("Y", lambda e, s=s, hv=hv: e.dma_start(
                out=hv[:, :], in_=hv_d[s]),
                deps=[vsig_ops[s - 2] if s >= 2 else None], ndma=1)
            vsg = plan.add("S", lambda e, hv=hv, vfm=vfm: [
                e.activation(out=vfm[:, o * SUP:(o + 1) * SUP],
                             in_=hv[:, o * SUP:(o + 1) * SUP], func=AF.Sigmoid,
                             scale=av_c[:, o:o + 1], bias=cv_c[:, o:o + 1])
                for o in range(NCH)
            ], deps=[ldhv, finv_c,
                     vtr_ops[s - 2][3] if s >= 2 else None])
            vsig_ops[s] = vsg
            vtrs = []
            for a in range(NSUB):
                bank, bdeps, bidx = pb_ring.acquire()
                bankb = bank[:, :].bitcast(BF16)
                tr = plan.add("T", lambda e, a=a, bankb=bankb, vfm=vfm: [
                    e.transpose(out=bankb[:, c * 128:(c + 1) * 128],
                                in_=vfm[:, c * SUP + a * 128:c * SUP + (a + 1) * 128],
                                identity=id_b[:, :])
                    for c in range(NCH)
                ], deps=[vsg] + bdeps)
                ev = plan.add("V", lambda e, a=a, bankb=bankb, vt=vt: e.tensor_copy(
                    out=vt[:, a * D:(a + 1) * D], in_=bankb[:, :]),
                    deps=[tr]
                    + ([vvmm_ops[s - 1]] if (a == 0 and s >= 1) else []))
                pb_ring.done(bidx, ev)
                vtrs.append(ev)
            vtr_ops[s] = vtrs

            first_in_seq = (s % 4 == 0)
            vvmm = plan.add("T", lambda e, s=s, seq=seq, first=first_in_seq, vt=vt: [
                e.matmul(out=vvps[seq][(0 if h == 0 else 64):(64 if h == 0 else 128),
                                       j * 64:(j + 1) * 64],
                         lhsT=vt[:, a * D + j * 128 + h * 64:a * D + j * 128 + (h + 1) * 64],
                         rhs=vt[:, a * D + j * 128 + h * 64:a * D + j * 128 + (h + 1) * 64],
                         start=(first and a == 0 and j == 0), stop=False,
                         skip_group_check=True,
                         tile_position=(0, 0 if h == 0 else 64))
                for a in range(NSUB) for j in range(NCH) for h in range(2)
            ], deps=vtrs)
            vvmm_ops[s] = vvmm

            if s % 4 == 3:
                evv = plan.add("V", lambda e, seq=seq: [
                    e.tensor_scalar(
                        out=VV[seq][:, :].rearrange("p (j e) -> p j e", j=NCH)[0:64, :, 0:64],
                        in0=vvps[seq][0:64, :].rearrange("p (j e) -> p j e", j=NCH),
                        scalar1=0.125, scalar2=None, op0=ALU.mult),
                    e.tensor_scalar(
                        out=VV[seq][:, :].rearrange("p (j e) -> p j e", j=NCH)[64:128, :, 64:128],
                        in0=vvps[seq][64:128, :].rearrange("p (j e) -> p j e", j=NCH),
                        scalar1=0.125, scalar2=None, op0=ALU.mult),
                ], deps=[vvmm, zeroVV])
                ev_vv_ops[seq] = evv

        def emit_2a_qk(s):
            qk = qk_ring[s % 3]
            ldhqk = plan.add("Y", lambda e, s=s: [
                e.dma_start(out=hq_in[s % 2][:, :], in_=hq_d[s]),
                e.dma_start(out=hk_in[s % 2][:, :], in_=hk_d[s]),
            ], deps=[qsig_ops[s - 2] if s >= 2 else None,
                     ksig_ops[s - 2] if s >= 2 else None], ndma=2)
            qkm_ch[s] = [None] * NCH
            qb = qbuf[s % 2]
            for o in range(NCH):
                qsig = plan.add("S", lambda e, s=s, o=o, qb=qb: e.activation(
                    out=qb[:, o * SUP:(o + 1) * SUP],
                    in_=hq_in[s % 2][:, o * SUP:(o + 1) * SUP], func=AF.Sigmoid,
                    scale=alph[:, o:o + 1], bias=coff[:, o:o + 1]),
                    deps=[ldhqk if o == 0 else None,
                          fin_c if (s == 0 and o == 0) else None,
                          qkm_ch[s - 2][o] if s >= 2 else None])
                ksig = plan.add("S", lambda e, s=s, o=o, qk=qk: e.activation(
                    out=qk[:, o * SUP:(o + 1) * SUP],
                    in_=hk_in[s % 2][:, o * SUP:(o + 1) * SUP], func=AF.Sigmoid,
                    scale=alph[:, 8 + o:9 + o], bias=coff[:, 8 + o:9 + o]),
                    deps=[attn_ch[s - 3][o] if s >= 3 else None])
                qkm_ch[s][o] = plan.add("V", lambda e, o=o, qk=qk, qb=qb: e.tensor_tensor(
                    out=qk[:, o * SUP:(o + 1) * SUP],
                    in0=qk[:, o * SUP:(o + 1) * SUP],
                    in1=qb[:, o * SUP:(o + 1) * SUP], op=ALU.mult),
                    deps=[qsig, ksig])
            qsig_ops[s] = qsig
            ksig_ops[s] = ksig

        st_out_ops = [None] * NSUP
        last_wp_mm = None
        last_tr3 = None
        pb_ring = Ring2(at_ps)
        epi2_by_sup = [None] * NSUP
        st3s_by_sup = [None] * NSUP
        nmrs_by_sup = [None] * NSUP
        ldx1_by_sup = [None] * NSUP
        last_ffn2_mm = None

        def emit_2b_front(s):
            nonlocal last_wp_mm
            seq = s // 4
            qk = qk_ring[s % 3]
            x1 = x1b[s % 2]
            ldx1_by_sup[s] = plan.add("Y", lambda e, s=s, x1=x1: e.dma_start(
                out=x1[:, :], in_=x1_d[s]),
                deps=[spx1_ops[s]]
                + (list(st_out_ops[s - 2]) if s >= 2 else []), ndma=1)

            ev_ats = []
            st3s = []
            attn_ch[s] = [None] * NCH
            for j in range(NCH):
                bank, bdeps, bidx = pb_ring.acquire()
                amm = plan.add("T", lambda e, j=j, seq=seq, bank=bank, qk=qk: e.matmul(
                    out=bank[:, :], lhsT=VV[seq][:, j * 128:(j + 1) * 128],
                    rhs=qk[:, j * SUP:(j + 1) * SUP], start=True, stop=True),
                    deps=[ev_vv_ops[seq], qkm_ch[s][j]] + bdeps
                    + ([last_wp_mm] if j == 0 else []))
                attn_ch[s][j] = amm
                aev = plan.add("S", lambda e, j=j, bank=bank: e.activation(
                    out=attn_sb[:, j * SUP:(j + 1) * SUP], in_=bank[:, :],
                    func=AF.Copy), deps=[amm])
                pb_ring.done(bidx, aev)
                ev_ats.append(aev)

            epi2 = []
            for a in range(NSUB):
                tile, tdeps, tok = pp2.acquire_full()
                mm = plan.add("T", lambda e, a=a, tile=tile: [
                    e.matmul(out=tile[:, oh * 512:(oh + 1) * 512],
                             lhsT=attn_sb[:, j * SUP + a * 128:j * SUP + (a + 1) * 128],
                             rhs=wp_sb[:, j * D + oh * 512:j * D + oh * 512 + 512],
                             start=(j == 0), stop=False)
                    for oh in range(2) for j in range(NCH)
                ] + [
                    e.matmul(out=tile[:, oh * 512:(oh + 1) * 512], lhsT=onesm[:, :],
                             rhs=bp_rep[:, oh * 512:(oh + 1) * 512], start=False, stop=True)
                    for oh in range(2)
                ], deps=ev_ats + tdeps + [wp_ready])
                last_wp_mm = mm
                epi = plan.add("V", lambda e, a=a, tile=tile, x1=x1: [
                    e.tensor_tensor(out=x1[:, a * D + oh * 512:a * D + (oh + 1) * 512],
                                    in0=tile[:, oh * 512:(oh + 1) * 512],
                                    in1=x1[:, a * D + oh * 512:a * D + (oh + 1) * 512],
                                    op=ALU.add)
                    for oh in range(2)
                ], deps=[mm, ldx1_by_sup[s]])
                pp2.done(tok, epi)
                epi2.append(epi)
                sta3 = plan.add("V", ln3[s % 2][3](x1, a), deps=[epi])
                cmb3 = plan.add("V", ln3[s % 2][5](a), deps=[sta3])
                st3s.append(plan.add("Q", ln3[s % 2][6](a), deps=[cmb3], sem="G"))
            epi2_by_sup[s] = epi2
            st3s_by_sup[s] = st3s
            nmrs_by_sup[s] = [
                plan.add("V", ln3[s % 2][8](a), deps=[st3s[a]])
                for a in range(NSUB)]

        def emit_2b_back(s):
            nonlocal last_tr3, last_ffn2_mm
            x1 = x1b[s % 2]
            x13 = x1[:, :].rearrange("p (a d) -> p a d", a=NSUB)
            epo = []
            mms3 = []
            for a in range(NSUB):
                fin3 = plan.add("S", ln3[s % 2][9](x1, ybuf2, a),
                                deps=[nmrs_by_sup[s][a],
                                      last_tr3 if a == 0 else None])
                bank, bdeps, bidx = pb_ring.acquire()
                bankb = bank[:, :].bitcast(BF16)
                tr = plan.add("T", lambda e, a=a, bankb=bankb: [
                    e.transpose(out=bankb[:, i * 128:(i + 1) * 128],
                                in_=ybuf2[:, a * D + i * 128:a * D + (i + 1) * 128],
                                identity=id_b[:, :])
                    for i in range(NCH)
                ], deps=[fin3] + bdeps)
                ev = plan.add("V", lambda e, a=a, bankb=bankb: e.tensor_copy(
                    out=y3T[:, :].rearrange("p (i t) -> p i t", i=NCH)[
                        :, :, (a % 2) * 128:(a % 2) * 128 + 128],
                    in_=bankb[:, :].rearrange("p (i t) -> p i t", i=NCH)),
                    deps=[tr] + ([mms3[a - 2]] if a >= 2 else []))
                pb_ring.done(bidx, ev)
                if a == 3:
                    last_tr3 = ev
                tile, tdeps, tok = pp2.acquire_full()
                mm = plan.add("T", lambda e, a=a, tile=tile: [
                    e.matmul(out=tile[:, oh * 512:(oh + 1) * 512],
                             lhsT=y3T[:, i * 256 + (a % 2) * 128:
                                      i * 256 + (a % 2) * 128 + 128],
                             rhs=w_half[:, i * D + oh * 512:i * D + oh * 512 + 512],
                             start=(i == 0), stop=False)
                    for oh in range(2) for i in range(NCH)
                ] + [
                    e.matmul(out=tile[:, oh * 512:(oh + 1) * 512], lhsT=onesm[:, :],
                             rhs=b_rep[:, oh * 512:(oh + 1) * 512], start=False, stop=True)
                    for oh in range(2)
                ], deps=[ev] + tdeps)
                mms3.append(mm)
                last_ffn2_mm = mm
                epi = plan.add("V", lambda e, a=a, tile=tile, x1=x1: [
                    e.tensor_tensor(out=x1[:, a * D + oh * 512:a * D + (oh + 1) * 512],
                                    in0=tile[:, oh * 512:(oh + 1) * 512],
                                    in1=x1[:, a * D + oh * 512:a * D + (oh + 1) * 512],
                                    op=ALU.add)
                    for oh in range(2)
                ], deps=[mm])
                pp2.done(tok, epi)
                epo.append(epi)

            st_a = plan.add("Y", lambda e, s=s, x13=x13: e.dma_start(
                out=out_ch[:, s * NSUB:s * NSUB + 2, :], in_=x13[:, 0:2, :]),
                deps=epo[0:2], ndma=1)
            st_b = plan.add("Q", lambda e, s=s, x13=x13: e.dma_start(
                out=out_ch[:, s * NSUB + 2:s * NSUB + 4, :], in_=x13[:, 2:4, :]),
                deps=epo[2:4], ndma=1)
            st_out_ops[s] = (st_a, st_b)

        wp_ready = w_ready["wp"]
        emit_2a_v(0)
        emit_2a_v(1)
        emit_2a_v(2)
        emit_2a_v(3)
        emit_fin_qk()
        emit_2a_qk(0)
        emit_2a_qk(1)
        emit_2b_front(0)
        emit_2a_v(4)
        emit_2a_qk(2)
        emit_2b_front(1)
        emit_2b_back(0)
        emit_2a_v(5)
        emit_2a_qk(3)
        emit_2b_front(2)
        emit_2b_back(1)
        emit_2a_v(6)
        emit_2a_qk(4)
        emit_2b_front(3)
        emit_2b_back(2)
        emit_2a_v(7)
        emit_2a_qk(5)
        pb_ring.slots.extend(vvps)
        pb_ring.last.extend([ev_vv_ops[0], ev_vv_ops[1]])
        emit_2b_front(4)
        emit_2b_back(3)
        emit_2a_qk(6)
        emit_2b_front(5)
        emit_2b_back(4)
        emit_2a_qk(7)
        emit_2b_front(6)
        emit_2b_back(5)
        emit_2b_front(7)
        emit_2b_back(6)
        emit_2b_back(7)

        plan.add("Q", lambda e: e.nop(),
                 deps=list(st_out_ops[-1]) + list(st_out_ops[-2]), sem="G")

        ps2a.close()
        ps2b.close()
        ph2.close()

        plan.finalize()
        with contextlib.ExitStack() as es2:
            sems = {n: es2.enter_context(nc.semaphore(f"sem{n}"))
                    for n in ("T", "V", "S", "Y", "Q", "G", "Z", "W",
                              "Cv", "Cq", "cT", "cV", "cS")}
            block = es2.enter_context(nc.Block())

            @block.sync
            def _(eng):
                plan.run(eng, "Y", sems)

            @block.tensor
            def _(eng):
                plan.run(eng, "T", sems)

            @block.vector
            def _(eng):
                plan.run(eng, "V", sems)

            @block.scalar
            def _(eng):
                plan.run(eng, "S", sems)

            @block.gpsimd
            def _(eng):
                plan.run(eng, "Q", sems)

    nc.reset()
    return nc


_prog_cache = {}


def _get_program(debug=DEBUG_OUTS):
    key = bool(debug)
    if key not in _prog_cache:
        _prog_cache[key] = build_program(debug)
    return _prog_cache[key]


def _prepare(L0_attn_x, L0_attn_Wq, L0_attn_gq, L0_attn_bq, L0_attn_Wk, L0_attn_gk,
             L0_attn_bk, L0_attn_Wv, L0_attn_gv, L0_attn_bv, L0_attn_Wp, L0_attn_bp,
             L0_ffn_W, L0_ffn_b, L0_ffn_lora_down, L0_ffn_lora_up, L0_ffn_down_aux,
             L0_ffn_up_aux):
    f32 = np.float32
    x = np.ascontiguousarray(np.asarray(L0_attn_x, dtype=f32))
    wqT = np.ascontiguousarray(np.asarray(L0_attn_Wq, dtype=f32).T)
    wkT = np.ascontiguousarray(np.asarray(L0_attn_Wk, dtype=f32).T)
    wvT = np.ascontiguousarray(np.asarray(L0_attn_Wv, dtype=f32).T)
    wpT = np.ascontiguousarray(np.asarray(L0_attn_Wp, dtype=f32).T)
    wfT = np.ascontiguousarray(np.asarray(L0_ffn_W, dtype=f32).T)
    lora_up = np.ascontiguousarray(np.asarray(L0_ffn_lora_up, dtype=f32))
    up_auxT = np.ascontiguousarray(np.asarray(L0_ffn_up_aux, dtype=f32).T)
    down_aux = np.ascontiguousarray(np.asarray(L0_ffn_down_aux, dtype=f32))
    lora_downT = np.ascontiguousarray(np.asarray(L0_ffn_lora_down, dtype=f32).T)
    col = lambda v: np.ascontiguousarray(np.asarray(v, dtype=f32).reshape(8, 128).T)
    gb16 = np.ascontiguousarray(np.concatenate(
        [col(L0_attn_gq), col(L0_attn_gk), col(L0_attn_bq), col(L0_attn_bk),
         col(L0_attn_gv), col(L0_attn_bv)], axis=1))
    bcols = np.ascontiguousarray(np.concatenate(
        [col(L0_ffn_b), col(L0_attn_bp)], axis=1))
    eye = np.eye(128, dtype=f32)

    shared = dict(wqT=wqT, wkT=wkT, wvT=wvT, wpT=wpT, wfT=wfT, lora_up=lora_up,
                  up_auxT=up_auxT, down_aux=down_aux, lora_downT=lora_downT,
                  gb16=gb16, bcols=bcols, eye=eye)
    in_maps = []
    for c in range(NCORE):
        m = dict(shared)
        m["x"] = np.ascontiguousarray(x[2 * c:2 * c + 2].reshape(T_CORE, D))
        in_maps.append(m)
    return in_maps


def kernel(**inputs):
    in_maps = _prepare(**inputs)
    nc = _get_program()
    res = run_bass_kernel_spmd(nc, in_maps, core_ids=list(range(NCORE)))
    out = np.concatenate([np.asarray(r["out"]).reshape(2, 2048, D)
                          for r in res.results], axis=0)
    return out.astype(np.float32)


def kernel_debug(**inputs):
    in_maps = _prepare(**inputs)
    nc = _get_program(debug=True)
    res = run_bass_kernel_spmd(nc, in_maps, core_ids=list(range(NCORE)))
    shards = [np.asarray(r["out"]) for r in res.results]
    return shards, res.results
